# version 08
# baseline (speedup 1.0000x reference)
import numpy as np
import ml_dtypes
import contextlib

import concourse.bass as bass
import concourse.mybir as mybir
from concourse.bass_utils import run_bass_kernel_spmd
from concourse.alu_op_type import AluOpType
from concourse.aot_env import allow_ici_all_to_all

F32 = mybir.dt.float32
BF16 = mybir.dt.bfloat16
AF = mybir.ActivationFunctionType
OP = AluOpType
bf16 = ml_dtypes.bfloat16

NCORE = 8
T = 8192
D = 2048
HD = 128
NH = 2
CH = 512
NCH = T // CH
INTER = 5504
NIT = INTER // 128
NDM = D // 128
TOK = 2 * CH
NTS = TOK // 128
EPS = 1e-5
OSCALE = float(D) ** -0.5
C_SHIFT = 0.0

ENGS = ("pe", "act", "dve", "gp", "sp")


class Builder:
    def __init__(self, nc):
        self.nc = nc
        self.ops = {e: [] for e in ENGS}
        self.sems = {}
        self.counts = {}
        self.es = contextlib.ExitStack()

    def sem(self, name):
        if name not in self.sems:
            self.sems[name] = self.es.enter_context(self.nc.semaphore(name))
            self.counts[name] = 0
        return self.sems[name]

    def val(self, name):
        return self.counts.get(name, 0)

    def emit(self, eng, fn, waits=(), inc=(), dma=False):
        step = 16 if dma else 1
        out_vals = {}
        incs = []
        for s in inc:
            self.sem(s)
            self.counts[s] += step
            out_vals[s] = self.counts[s]
            incs.append((self.sems[s], step))
        merged = {}
        for (s, v) in waits:
            if v > 0:
                merged[s] = max(merged.get(s, 0), v)
        wl = [(self.sem(s), v) for s, v in merged.items()]
        self.ops[eng].append((wl, fn, incs))
        return out_vals

    def build_block(self):
        ops = self.ops
        with self.nc.Block() as block:
            def runner(eng_name):
                def _run(eng):
                    for wl, fn, incs in ops[eng_name]:
                        for sem, v in wl:
                            eng.wait_ge(sem, v)
                        instr = fn(eng)
                        for sem, step in incs:
                            if instr is None:
                                instr = eng.nop()
                            instr = instr.then_inc(sem, step)
                return _run
            block.tensor(runner("pe"))
            block.scalar(runner("act"))
            block.vector(runner("dve"))
            block.gpsimd(runner("gp"))
            block.sync(runner("sp"))


class Arena:

    def __init__(self, tensor, ncols, align=16):
        self.t = tensor
        self.n = ncols
        self.align = align
        self.live = {}

    def alloc(self, name, cols):
        assert name not in self.live
        al = self.align
        taken = sorted(self.live.values())
        start = 0
        for s, ln in taken:
            if start + cols <= s:
                break
            start = -(-(s + ln) // al) * al
        assert start + cols <= self.n, (
            f"arena overflow: {name} needs {cols} @ {start}, cap {self.n}")
        self.live[name] = (start, cols)
        return start

    def free(self, *names):
        for nm in names:
            del self.live[nm]

    def ap(self, name, c0=0, cols=None, p0=0, pn=128):
        s, ln = self.live[name]
        if cols is None:
            cols = ln - c0
        assert c0 + cols <= ln, (name, c0, cols, ln)
        return self.t[p0:p0 + pn, s + c0:s + c0 + cols]


def build_nc():
    nc = bass.Bass("TRN2", target_bir_lowering=False)

    def din(name, shape, dt=F32):
        return nc.dram_tensor(name, shape, dt, kind="ExternalInput").ap()

    xT = din("xT", [D, T])
    x_own = din("x_own", [TOK, D])
    wq = din("wq", [D, NH * HD])
    wk = din("wk", [D, HD])
    wv = din("wv", [D, HD])
    wo = din("wo", [D, D])
    bq = din("bq", [128, NH])
    bk = din("bk", [HD, 1])
    bv = din("bv", [HD, 1])
    bo = din("bo", [1, D])
    cosd = din("cosd", [128, T])
    sind = din("sind", [128, T])
    maskd = din("maskd", [128, 4 * CH], BF16)
    lutd = din("lutd", [128, 16])
    gidx = din("gidx", [D // NCORE, INTER], BF16)
    uidx = din("uidx", [D // NCORE, INTER], BF16)
    didx = din("didx", [INTER // NCORE, D], BF16)
    out_d = nc.dram_tensor("out", [TOK, D], F32, kind="ExternalOutput").ap()

    ag_g_in = nc.dram_tensor("ag_g_in", [D // NCORE, INTER], BF16).ap()
    ag_u_in = nc.dram_tensor("ag_u_in", [D // NCORE, INTER], BF16).ap()
    ag_d_in = nc.dram_tensor("ag_d_in", [INTER // NCORE, D], BF16).ap()
    ag_g = nc.dram_tensor("ag_g", [D, INTER], BF16, addr_space="Shared").ap()
    ag_u = nc.dram_tensor("ag_u", [D, INTER], BF16, addr_space="Shared").ap()
    ag_d = nc.dram_tensor("ag_d", [INTER, D], BF16, addr_space="Shared").ap()
    a2a_in = [nc.dram_tensor(f"a2a_in{i}", [NCORE * NH * HD, CH], BF16).ap()
              for i in range(2)]
    a2a_out = [nc.dram_tensor(f"a2a_out{i}", [NCORE * NH * HD, CH], BF16).ap()
               for i in range(2)]

    B = Builder(nc)
    es = B.es

    BF_COLS = 81800
    F32_COLS = 12000
    abf = Arena(es.enter_context(nc.sbuf_tensor("abf", [128, BF_COLS], BF16)),
                BF_COLS, align=16)
    af = Arena(es.enter_context(nc.sbuf_tensor("af32", [128, F32_COLS], F32)),
               F32_COLS, align=8)
    pb = [es.enter_context(nc.psum_tensor(f"pb{i}", [128, 512], F32))
          for i in range(8)]

    af.alloc("lut", 16)
    af.alloc("bq", NH)
    af.alloc("bk", 1)
    af.alloc("bv", 1)
    af.alloc("bo_bc", D)
    af.alloc("sonesf", 128)
    af.alloc("ln", 16)
    af.alloc("epsv", 1)
    af.alloc("bn6", 24)
    abf.alloc("masks", 4 * CH)
    abf.alloc("sones", 1)

    def d_sp(sem, dst, src, waits=()):
        v = B.emit("sp", lambda e, dst=dst, src=src: e.dma_start(out=dst, in_=src),
                   waits=waits, inc=(sem,), dma=True)
        return (sem, v[sem])

    def d_gp(sem, dst, src, waits=()):
        v = B.emit("gp", lambda e, dst=dst, src=src: e.dma_start(out=dst, in_=src),
                   waits=waits, inc=(sem,), dma=True)
        return (sem, v[sem])

    def tsp(sem, dst, src, waits=()):
        v = B.emit("sp", lambda e, dst=dst, src=src: e.dma_start(
            out=dst, in_=src, transpose=True), waits=waits, inc=(sem,), dma=True)
        return (sem, v[sem])

    def mm(out, lhsT, rhs, start, stop, waits=(), inc=()):
        return B.emit("pe", lambda e, out=out, lhsT=lhsT, rhs=rhs, start=start,
                      stop=stop: e.matmul(out, lhsT, rhs, start=start, stop=stop),
                      waits=waits, inc=inc)

    def act(outp, inp, func, bias, scale, waits=()):
        return B.emit("act", lambda e, outp=outp, inp=inp, func=func, bias=bias,
                      scale=scale: e.activation(outp, inp, func, bias=bias,
                                                scale=scale),
                      waits=waits, inc=("act",))

    def tt(outp, in0, in1, op, waits=()):
        return B.emit("dve", lambda e, outp=outp, in0=in0, in1=in1, op=op:
                      e.tensor_tensor(out=outp, in0=in0, in1=in1, op=op),
                      waits=waits, inc=("dve",))

    def dve_chain():
        return ("dve", B.val("dve"))

    d_sp("s_ld", af.ap("lut"), lutd)
    d_sp("s_ld", af.ap("bq"), bq)
    d_sp("s_ld", af.ap("bk"), bk)
    d_sp("s_ld", af.ap("bv"), bv)
    d_sp("s_ld", abf.ap("masks"), maskd)
    ld0 = d_sp("s_ld", af.ap("bo_bc", 0, D, pn=1), bo)

    abf.alloc("xbuf", 2 * NDM * CH)
    abf.alloc("wq", NDM * NH * HD)
    abf.alloc("wk", NDM * HD)
    abf.alloc("wv", NDM * HD)
    af.alloc("cc", 2 * CH)
    af.alloc("ss", 2 * CH)
    af.alloc("ktmp", 2 * CH)
    af.alloc("qtmp", 4 * CH)
    af.alloc("ra", 2 * CH)
    af.alloc("rb", 2 * CH)
    abf.alloc("vbfT", 2 * CH)
    abf.alloc("kT", T)
    abf.alloc("qT", NH * T)
    abf.alloc("v", T)
    abf.alloc("p", 4 * CH)
    abf.alloc("o", 2 * CH)
    af.alloc("lrow", CH)
    af.alloc("lrec", CH)
    af.alloc("bc", 2 * CH)
    abf.alloc("idx", INTER)
    abf.alloc("wacc", INTER)
    abf.alloc("wcon", INTER)

    d_gp("s_ldw", abf.ap("wq"), wq.rearrange("(n p) m -> p n m", p=128))
    d_gp("s_ldw", abf.ap("wk"), wk.rearrange("(n p) m -> p n m", p=128))
    ldw = d_gp("s_ldw", abf.ap("wv"), wv.rearrange("(n p) m -> p n m", p=128))
    B.emit("dve", lambda e, a=abf.ap("sones"): e.memset(a, 1.0), inc=("dve",))
    B.emit("dve", lambda e, a=af.ap("epsv"): e.memset(a, EPS), inc=("dve",))
    ones_init = B.emit("dve", lambda e, a=af.ap("sonesf", pn=1): e.memset(a, OSCALE),
                       inc=("dve",))["dve"]

    def wtiles(nrows):
        tl, r = [], 0
        while r < nrows:
            tl.append((r, min(128, nrows - r)))
            r += 128
        return tl

    wb_done = {}
    gather_tiles = []
    for mat, idx_d, ag_in_d, fd in (("g", gidx, ag_g_in, INTER),
                                    ("u", uidx, ag_u_in, INTER),
                                    ("d", didx, ag_d_in, D)):
        tl = wtiles(idx_d.shape[0])
        for ti, (r0, nr) in enumerate(tl):
            gather_tiles.append((mat, idx_d, ag_in_d, fd, r0, nr,
                                 ti == len(tl) - 1))

    _gstate = {"prev_store": None, "emitted": 0}

    def emit_gather_tile(k):
        mat, idx_d, ag_in_d, fd, r0, nr, last = gather_tiles[k]
        idx_ap = abf.ap("idx", 0, fd, pn=nr)
        acc_ap = abf.ap("wacc", 0, fd, pn=nr)
        con_ap = abf.ap("wcon", 0, fd, pn=nr)
        ps = _gstate["prev_store"]
        lv = d_gp("s_idx", idx_ap, idx_d[r0:r0 + nr, :],
                  waits=[ps] if ps else [])
        B.emit("dve", lambda e, o=acc_ap, i=idx_ap, s2=af.ap("lut", 0, 1, pn=nr):
               e.tensor_scalar(out=o, in0=i, scalar1=0.0, scalar2=s2,
                               op0=OP.is_equal, op1=OP.mult),
               waits=[lv, ld0, dve_chain()], inc=("dve",))
        for j in range(1, 16):
            B.emit("dve", lambda e, o=con_ap, i=idx_ap, j=j,
                   s2=af.ap("lut", j, 1, pn=nr):
                   e.tensor_scalar(out=o, in0=i, scalar1=float(j), scalar2=s2,
                                   op0=OP.is_equal, op1=OP.mult),
                   waits=[dve_chain()], inc=("dve",))
            tt(acc_ap, acc_ap, con_ap, OP.add, waits=[dve_chain()])
        _gstate["prev_store"] = d_gp("s_wst_" + mat, ag_in_d[r0:r0 + nr, :],
                                     acc_ap, waits=[dve_chain()])
        if last:
            wb_done[mat] = _gstate["prev_store"]

    x_dma = {}
    pe_xfree = {}
    act_evac = {}
    kq_done = {}
    vT_done = {}

    for tc in range(NCH):
        par = tc % 2
        w = [("pe", pe_xfree[tc - 2])] if tc >= 2 else []
        x_dma[tc] = d_gp("s_x%d" % par, abf.ap("xbuf", par * NDM * CH, NDM * CH),
                         xT[:, tc * CH:(tc + 1) * CH].rearrange(
                             "(n p) m -> p n m", p=128), waits=w)
        csw = [("dve", kq_done[tc - 2])] if tc >= 2 else []
        d_sp("s_cs%d" % par, af.ap("cc", par * CH, CH),
             cosd[:, tc * CH:(tc + 1) * CH], waits=csw)
        cs_val = d_sp("s_cs%d" % par, af.ap("ss", par * CH, CH),
                      sind[:, tc * CH:(tc + 1) * CH])

        pew = [x_dma[tc], ldw]
        if tc >= 2:
            pew += [("act", act_evac[("k", tc - 2)]),
                    ("act", act_evac[("v", tc - 2)])]
        if tc >= 1:
            pew.append(("act", act_evac[("q1", tc - 1)]))
        for dm in range(NDM):
            iv = mm(pb[par][:, :], abf.ap("wk", dm * HD, HD),
                    abf.ap("xbuf", par * NDM * CH + dm * CH, CH),
                    start=(dm == 0), stop=(dm == NDM - 1),
                    waits=pew if dm == 0 else (),
                    inc=("pe",) if dm == NDM - 1 else ())
        pe_k = iv["pe"]
        for dm in range(NDM):
            iv = mm(pb[2 + par][:, :], abf.ap("wv", dm * HD, HD),
                    abf.ap("xbuf", par * NDM * CH + dm * CH, CH),
                    start=(dm == 0), stop=(dm == NDM - 1),
                    inc=("pe",) if dm == NDM - 1 else ())
        pe_v = iv["pe"]
        pe_q = {}
        for h in range(NH):
            for dm in range(NDM):
                iv = mm(pb[4 + h][:, :], abf.ap("wq", dm * NH * HD + h * HD, HD),
                        abf.ap("xbuf", par * NDM * CH + dm * CH, CH),
                        start=(dm == 0), stop=(dm == NDM - 1),
                        inc=("pe",) if dm == NDM - 1 else ())
            pe_q[h] = iv["pe"]
        pe_xfree[tc] = pe_q[NH - 1]

        roped_prev = kq_done.get(tc - 1, 0)
        act(af.ap("ktmp", 0, CH, p0=0, pn=64), pb[par][0:64, :], AF.Identity,
            af.ap("bk", p0=0, pn=64), 1.0,
            waits=[("pe", pe_k), ("dve", roped_prev), ld0])
        act_evac[("k", tc)] = act(
            af.ap("ktmp", CH, CH, p0=0, pn=64), pb[par][64:128, :], AF.Identity,
            af.ap("bk", p0=64, pn=64), 1.0)["act"]
        vw = [("pe", pe_v)]
        if tc >= 2:
            vw.append(vT_done[tc - 2])
        act_evac[("v", tc)] = act(abf.ap("vbfT", par * CH, CH), pb[2 + par][:, :],
                                  AF.Identity, af.ap("bv"), 1.0, waits=vw)["act"]
        for h in range(NH):
            act(af.ap("qtmp", h * 2 * CH, CH, p0=0, pn=64), pb[4 + h][0:64, :],
                AF.Identity, af.ap("bq", h, 1, p0=0, pn=64), 1.0,
                waits=[("pe", pe_q[h]), ("dve", roped_prev)])
            act_evac[(f"q{h}", tc)] = act(
                af.ap("qtmp", h * 2 * CH + CH, CH, p0=0, pn=64),
                pb[4 + h][64:128, :], AF.Identity,
                af.ap("bq", h, 1, p0=64, pn=64), 1.0)["act"]

        for j in range(CH // 128):
            vts = tsp("s_vt%d" % par, abf.ap("v", tc * CH + j * 128, 128),
                      abf.ap("vbfT", par * CH + j * 128, 128),
                      waits=[("act", act_evac[("v", tc)])] if j == 0 else ())
        vT_done[tc] = vts

        def rope(lo_ap, hi_ap, dst_name, dst_off, wait_act):
            t1 = af.ap("ra", par * CH, CH, pn=64)
            t2 = af.ap("rb", par * CH, CH, pn=64)
            cc64 = af.ap("cc", par * CH, CH, pn=64)
            ss64 = af.ap("ss", par * CH, CH, pn=64)
            tt(t1, lo_ap, cc64, OP.mult,
               waits=[("act", wait_act), cs_val, dve_chain()])
            tt(t2, hi_ap, ss64, OP.mult, waits=[dve_chain()])
            tt(abf.ap(dst_name, dst_off, CH, p0=0, pn=64), t1, t2, OP.subtract,
               waits=[dve_chain()])
            tt(t1, lo_ap, ss64, OP.mult, waits=[dve_chain()])
            tt(t2, hi_ap, cc64, OP.mult, waits=[dve_chain()])
            tt(abf.ap(dst_name, dst_off, CH, p0=64, pn=64), t1, t2, OP.add,
               waits=[dve_chain()])

        rope(af.ap("ktmp", 0, CH, pn=64), af.ap("ktmp", CH, CH, pn=64),
             "kT", tc * CH, act_evac[("k", tc)])
        for h in range(NH):
            rope(af.ap("qtmp", h * 2 * CH, CH, pn=64),
                 af.ap("qtmp", h * 2 * CH + CH, CH, pn=64),
                 "qT", h * T + tc * CH, act_evac[(f"q{h}", tc)])
        kq_done[tc] = B.val("dve")

    proj_done_dve = B.val("dve")
    vt_final = [("s_vt0", B.val("s_vt0")), ("s_vt1", B.val("s_vt1"))]

    bo_w = [("pe", pe_xfree[NCH - 1]), ("act", act_evac[("q1", NCH - 1)]),
            ("dve", ones_init), ld0]
    bo_bc_done = 0
    for c4 in range(4):
        pv = mm(pb[4][:, :], af.ap("sonesf", 0, 128, pn=1),
                af.ap("bo_bc", c4 * 512, 512, pn=1), True, True,
                waits=bo_w if c4 == 0 else [("dve", bo_bc_done)],
                inc=("pe",))["pe"]
        bo_bc_done = B.emit("dve", lambda e, o=af.ap("bo_bc", c4 * 512, 512),
                            i=pb[c4 * 0 + 4][:, :]:
                            e.tensor_scalar(out=o, in0=i, scalar1=float(D) ** 0.5,
                                            scalar2=None, op0=OP.mult,
                                            op1=OP.bypass),
                            waits=[("pe", pv), dve_chain()], inc=("dve",))["dve"]

    NKT = CH // 128
    o_done = {}
    pslot_free = {}
    bank_free_qk = {}
    pv_free = {}
    bc_free = {}
    l_free = 0
    bv_prev = 0
    bc_prev = 0
    o_store_seq = {}
    attn_barrier = [("dve", max(proj_done_dve, bo_bc_done)), ld0] + vt_final
    qk_count = 0
    seq = 0
    for qc in range(NCH):
        for h in range(NH):
            hp = seq % 2
            nkt = (qc + 1) * NKT
            for kt in range(nkt):
                qkpar = qk_count % 4
                slot = qk_count % 4
                p_ap = abf.ap("p", slot * CH, CH)
                w = list(attn_barrier) if qk_count == 0 else []
                if qkpar in bank_free_qk:
                    w.append(("act", bank_free_qk[qkpar]))
                qv = mm(pb[qkpar][:, :], abf.ap("kT", kt * 128, 128),
                        abf.ap("qT", h * T + qc * CH, CH), True, True,
                        waits=w, inc=("pe",))["pe"]
                ew = [("pe", qv)]
                if slot in pslot_free:
                    ew.append(pslot_free[slot])
                av = B.emit("act", lambda e, o=p_ap, i=pb[qkpar][:, :]:
                            e.activation(o, i, AF.Exp, bias=-C_SHIFT, scale=1.0),
                            waits=ew, inc=("act",))["act"]
                bank_free_qk[qkpar] = av
                pdep = ("act", av)
                if kt >= nkt - NKT:
                    j = kt - (nkt - NKT)
                    mv = tt(p_ap, p_ap, abf.ap("masks", j * CH, CH), OP.mult,
                            waits=[("act", av), dve_chain()])["dve"]
                    pdep = ("dve", mv)
                avw = [pdep]
                if kt == 0 and hp in pv_free:
                    avw.append(("dve", pv_free[hp]))
                mm(pb[6 + hp][:, :], abf.ap("v", kt * 128, 128), p_ap,
                   start=(kt == 0), stop=(kt == nkt - 1), waits=avw)
                lw = [("dve", l_free)] if (kt == 0 and l_free) else []
                lv = mm(pb[5][0:1, :], abf.ap("sones"), p_ap,
                        start=(kt == 0), stop=(kt == nkt - 1), waits=lw,
                        inc=("pe",))["pe"]
                pslot_free[slot] = ("pe", lv)
                qk_count += 1
            B.emit("dve", lambda e, o=af.ap("lrow", 0, CH, pn=1),
                   i=pb[5][0:1, :]: e.tensor_copy(o, i),
                   waits=[("pe", lv), dve_chain()], inc=("dve",))
            l_free = B.emit("dve", lambda e, o=af.ap("lrec", 0, CH, pn=1),
                            i=af.ap("lrow", 0, CH, pn=1): e.reciprocal(o, i),
                            waits=[dve_chain(), ("pe", bv_prev)], inc=("dve",))["dve"]
            bw = [("dve", l_free), ("dve", bc_prev)]
            bv_ = mm(pb[4][:, :], af.ap("sonesf", 0, 128, pn=1),
                     af.ap("lrec", 0, CH, pn=1), True, True, waits=bw,
                     inc=("pe",))["pe"]
            bv_prev = bv_
            bc_free[hp] = B.emit("dve", lambda e, o=af.ap("bc", hp * CH, CH),
                                 i=pb[4][:, :]: e.tensor_copy(o, i),
                                 waits=[("pe", bv_), dve_chain()],
                                 inc=("dve",))["dve"]
            bc_prev = bc_free[hp]
            ow = [("dve", bc_free[hp])]
            if seq >= 2:
                ow.append(o_store_seq[seq - 2])
            om = tt(abf.ap("o", hp * CH, CH), pb[6 + hp][:, :],
                    af.ap("bc", hp * CH, CH), OP.mult, waits=ow)["dve"]
            pv_free[hp] = om
            half = qc // (NCH // 2)
            dest = qc % (NCH // 2)
            ov = d_sp("s_o%d%d" % (half, hp),
                      a2a_in[half][dest * NH * HD + h * HD:
                                   dest * NH * HD + (h + 1) * HD, :],
                      abf.ap("o", hp * CH, CH), waits=[("dve", om)])
            o_done[(h, qc)] = ov
            o_store_seq[seq] = ov
            ngt = len(gather_tiles)
            want = min(ngt, (seq + 1) * ngt // 24)
            while _gstate["emitted"] < want:
                emit_gather_tile(_gstate["emitted"])
                _gstate["emitted"] += 1
            seq += 1

    while _gstate["emitted"] < len(gather_tiles):
        emit_gather_tile(_gstate["emitted"])
        _gstate["emitted"] += 1

    cc_vals = {}

    def collective(kind, ins, outs, waits):
        def fn(e, kind=kind, ins=ins, outs=outs):
            if kind == "AllToAll":
                with allow_ici_all_to_all():
                    return e.collective_compute(
                        kind, OP.bypass, replica_groups=[list(range(NCORE))],
                        ins=ins, outs=outs)
            return e.collective_compute(
                kind, OP.bypass, replica_groups=[list(range(NCORE))],
                ins=ins, outs=outs)
        return B.emit("gp", fn, waits=waits, inc=("cc",))["cc"]

    cc_vals["g"] = collective("AllGather", [ag_g_in.opt()], [ag_g.opt()],
                              [wb_done["g"]])
    cc_vals["u"] = collective("AllGather", [ag_u_in.opt()], [ag_u.opt()],
                              [wb_done["u"]])
    cc_vals["a1"] = collective(
        "AllToAll", [a2a_in[0].opt()], [a2a_out[0].opt()],
        [("s_o00", B.val("s_o00")), ("s_o01", B.val("s_o01"))])
    cc_vals["a2"] = collective(
        "AllToAll", [a2a_in[1].opt()], [a2a_out[1].opt()],
        [("s_o10", B.val("s_o10")), ("s_o11", B.val("s_o11"))])
    cc_vals["d"] = collective("AllGather", [ag_d_in.opt()], [ag_d.opt()],
                              [wb_done["d"]])

    abf.free("xbuf", "wq", "wk", "wv", "vbfT", "kT", "qT", "v", "p", "o",
             "idx", "wacc", "wcon")
    af.free("cc", "ss", "ktmp", "qtmp", "ra", "rb", "lrow", "lrec", "bc")

    abf.alloc("oown", NDM * TOK)
    abf.alloc("woslab", 2 * D)
    abf.alloc("x1bf", NTS * D)
    abf.alloc("x1T", NDM * TOK)
    af.alloc("xbo", 2 * D)
    af.alloc("r1", 2 * D)

    for dt in range(NDM):
        for half in range(2):
            d_sp("s_ol", abf.ap("oown", dt * TOK + half * CH, CH),
                 a2a_out[half][dt * 128:(dt + 1) * 128, :],
                 waits=[("cc", cc_vals["a1" if half == 0 else "a2"])])
    oload_all = ("s_ol", B.val("s_ol"))

    NOROUND = NTS // 2
    wo_pe = {}
    evac_done = {}
    x1_done = {}
    xbo_add = {}

    for rnd in range(NOROUND):
        ts0, ts1 = 2 * rnd, 2 * rnd + 1
        for ts in (ts0, ts1):
            pr = ts % 2
            w = []
            if ts >= 2:
                w.append(("dve", evac_done[(rnd - 1, (ts % 2) * 4 + 3)]))
            xv = d_sp("s_xo%d" % pr, af.ap("xbo", pr * D, D),
                      x_own[ts * 128:(ts + 1) * 128, :], waits=w)
            xbo_add[ts] = tt(af.ap("xbo", pr * D, D), af.ap("xbo", pr * D, D),
                             af.ap("bo_bc"), OP.add,
                             waits=[xv, ("dve", max(B.val("dve"),
                                                    bo_bc_done))])["dve"]
        for dt in range(NDM):
            par = dt % 2
            w = []
            key_prev = (rnd, dt - 2) if dt >= 2 else (rnd - 1, dt + NDM - 2)
            if key_prev in wo_pe:
                w.append(("pe", wo_pe[key_prev]))
            gval = d_gp("s_wo%d" % par, abf.ap("woslab", par * D, D),
                        wo[dt * 128:(dt + 1) * 128, :], waits=w)
            for i, ts in enumerate((ts0, ts1)):
                mw = [gval]
                if dt == 0:
                    mw.append(oload_all)
                    for c4 in range(4):
                        if (rnd - 1, i * 4 + c4) in evac_done:
                            mw.append(("dve", evac_done[(rnd - 1, i * 4 + c4)]))
                for c4 in range(4):
                    iv = mm(pb[i * 4 + c4][:, :],
                            abf.ap("oown", dt * TOK + ts * 128, 128),
                            abf.ap("woslab", par * D + c4 * 512, 512),
                            start=(dt == 0), stop=(dt == NDM - 1),
                            waits=mw if c4 == 0 else (),
                            inc=("pe",) if (c4 == 3 and i == 1) else ())
            wo_pe[(rnd, dt)] = iv["pe"]
        mm_done = wo_pe[(rnd, NDM - 1)]
        for i, ts in enumerate((ts0, ts1)):
            pr = ts % 2
            for c4 in range(4):
                ev = tt(af.ap("r1", pr * D + c4 * 512, 512), pb[i * 4 + c4][:, :],
                        af.ap("xbo", pr * D + c4 * 512, 512), OP.add,
                        waits=[("pe", mm_done),
                               ("act", x1_done.get(ts - 2, 0)),
                               ("dve", max(B.val("dve"), xbo_add[ts]))])["dve"]
                evac_done[(rnd, i * 4 + c4)] = ev
            dvw = B.val("dve")
            for c4 in range(4):
                B.emit("dve", lambda e, o=af.ap("bn6", c4 * 6, 6),
                       i=af.ap("r1", pr * D + c4 * 512, 512): e.bn_stats(o, i),
                       waits=[("dve", dvw)] if c4 == 0 else (), inc=("dve",))
            B.emit("dve", lambda e, o=af.ap("ln", 0, 2), i=af.ap("bn6"):
                   e.bn_aggr(o, i), waits=[dve_chain()], inc=("dve",))
            sq = act(af.ap("ln", 2, 1), af.ap("ln", 1, 1), AF.Sqrt, af.ap("epsv"), 1.0,
                     waits=[dve_chain()])["act"]
            B.emit("dve", lambda e, o=af.ap("ln", 3, 1), i=af.ap("ln", 2, 1):
                   e.reciprocal(o, i), waits=[("act", sq), dve_chain()],
                   inc=("dve",))
            nm = B.emit("dve", lambda e, o=af.ap("ln", 4, 1), i=af.ap("ln", 0, 1),
                        s1=af.ap("ln", 3, 1):
                        e.tensor_scalar(out=o, in0=i, scalar1=s1, scalar2=-1.0,
                                        op0=OP.mult, op1=OP.mult),
                        waits=[dve_chain()], inc=("dve",))["dve"]
            x1_done[ts] = act(abf.ap("x1bf", ts * D, D), af.ap("r1", pr * D, D),
                              AF.Identity, af.ap("ln", 4, 1), af.ap("ln", 3, 1),
                              waits=[("dve", nm)])["act"]
            for dt in range(NDM):
                tv = tsp("s_x1t", abf.ap("x1T", dt * TOK + ts * 128, 128),
                         abf.ap("x1bf", ts * D + dt * 128, 128),
                         waits=[("act", x1_done[ts])] if dt == 0 else ())
    x1T_all = ("s_x1t", B.val("s_x1t"))

    abf.free("oown", "woslab")
    af.free("xbo", "r1")
    abf.alloc("hT", NIT * CH)
    abf.alloc("wgslab", 2 * D)
    abf.alloc("wuslab", 2 * D)
    abf.alloc("wdslab", 2 * D)
    af.alloc("sg", 2 * 512)
    af.alloc("r2", 2 * D)
    af.alloc("outb", 2 * D)

    out_stores = []
    r2_guard = {0: 0, 1: 0}
    gslab_pe = {}
    prev_down_pe = 0
    th_barrier_dve = 0
    th_barrier_act = 0
    for th in range(2):
        silu_done = {}
        hmul_done = {}
        for it in range(NIT):
            par = it % 2
            w = []
            if (th, it - 2) in gslab_pe:
                w.append(("pe", gslab_pe[(th, it - 2)]))
            elif (th - 1, NIT - 2 + it) in gslab_pe:
                w.append(("pe", gslab_pe[(th - 1, NIT - 2 + it)]))
            if th == 1 and it < 2:
                w.append(("pe", prev_down_pe))
            ccw = [("cc", cc_vals["g"])] if (th == 0 and it < 2) else []
            d_sp("s_wgu%d" % par, abf.ap("wgslab", par * D, D),
                 ag_g[:, it * 128:(it + 1) * 128].rearrange(
                     "(n p) m -> p n m", p=128), waits=w + ccw)
            ccw = [("cc", cc_vals["u"])] if (th == 0 and it < 2) else []
            slab_val = d_sp("s_wgu%d" % par, abf.ap("wuslab", par * D, D),
                            ag_u[:, it * 128:(it + 1) * 128].rearrange(
                                "(n p) m -> p n m", p=128), waits=ccw)
            mw = [slab_val, x1T_all]
            if th == 0 and it == 0:
                mw += [("dve", B.val("dve")), ("act", B.val("act"))]
            if th == 1 and it < 2:
                mw += [("dve", th_barrier_dve), ("act", th_barrier_act)]
            if (th, it - 2) in silu_done:
                mw += [("act", silu_done[(th, it - 2)]),
                       ("dve", hmul_done[(th, it - 2)])]
            for dm in range(NDM):
                mm(pb[par][:, :], abf.ap("wgslab", par * D + dm * 128, 128),
                   abf.ap("x1T", dm * TOK + th * CH, CH),
                   start=(dm == 0), stop=(dm == NDM - 1),
                   waits=mw if dm == 0 else ())
            for dm in range(NDM):
                iv = mm(pb[2 + par][:, :], abf.ap("wuslab", par * D + dm * 128, 128),
                        abf.ap("x1T", dm * TOK + th * CH, CH),
                        start=(dm == 0), stop=(dm == NDM - 1),
                        inc=("pe",) if dm == NDM - 1 else ())
            gmv = iv["pe"]
            gslab_pe[(th, it)] = gmv
            sw = [("pe", gmv)]
            if (th, it - 2) in hmul_done:
                sw.append(("dve", hmul_done[(th, it - 2)]))
            elif th == 1 and it < 2:
                sw.append(("dve", th_barrier_dve))
            silu_done[(th, it)] = act(af.ap("sg", par * 512, 512), pb[par][:, :],
                                      AF.Silu, 0.0, 1.0, waits=sw)["act"]
            hmul_done[(th, it)] = B.emit(
                "dve", lambda e, o=abf.ap("hT", it * CH, CH),
                i0=af.ap("sg", par * 512, 512), i1=pb[2 + par][:, :]:
                e.tensor_tensor(out=o, in0=i0, in1=i1, op=OP.mult),
                waits=[("act", silu_done[(th, it)]), dve_chain()],
                inc=("dve",))["dve"]
        hT_ready = max(hmul_done.values())
        last_silu = max(silu_done.values())

        dpair_pe = {}
        for pr2 in range(2):
            ts2a, ts2b = 2 * pr2, 2 * pr2 + 1
            for it in range(NIT):
                par = it % 2
                w = []
                if (pr2, it - 2) in dpair_pe:
                    w.append(("pe", dpair_pe[(pr2, it - 2)]))
                elif it < 2 and prev_down_pe:
                    w.append(("pe", prev_down_pe))
                ccw = ([("cc", cc_vals["d"])]
                       if (th == 0 and pr2 == 0 and it < 2) else [])
                sval = d_sp("s_wd%d" % par, abf.ap("wdslab", par * D, D),
                            ag_d[it * 128:(it + 1) * 128, :],
                            waits=w + ccw)
                mw = [sval, ("dve", hT_ready)]
                if it == 0:
                    mw += [("dve", B.val("dve")), ("act", B.val("act"))]
                for i2, ts2 in enumerate((ts2a, ts2b)):
                    for hc in range(4):
                        iv = mm(pb[i2 * 4 + hc][:, :],
                                abf.ap("hT", it * CH + ts2 * 128, 128),
                                abf.ap("wdslab", par * D + hc * 512, 512),
                                start=(it == 0), stop=(it == NIT - 1),
                                waits=mw if (i2 == 0 and hc == 0) else (),
                                inc=("pe",) if (i2 == 1 and hc == 3) else ())
                dpair_pe[(pr2, it)] = iv["pe"]
            down_done = dpair_pe[(pr2, NIT - 1)]
            prev_down_pe = down_done
            for i2, ts2 in enumerate((ts2a, ts2b)):
                ts = th * 4 + ts2
                pr = i2
                for hc in range(4):
                    tt(af.ap("r2", pr * D + hc * 512, 512), pb[i2 * 4 + hc][:, :],
                       abf.ap("x1bf", ts * D + hc * 512, 512), OP.add,
                       waits=[("pe", down_done), dve_chain(),
                              ("act", r2_guard[pr])])
                dvv = B.val("dve")
                for c4 in range(4):
                    B.emit("dve", lambda e, o=af.ap("bn6", c4 * 6, 6),
                           i=af.ap("r2", pr * D + c4 * 512, 512): e.bn_stats(o, i),
                           waits=[("dve", dvv)] if c4 == 0 else (), inc=("dve",))
                B.emit("dve", lambda e, o=af.ap("ln", 0, 2), i=af.ap("bn6"):
                       e.bn_aggr(o, i), waits=[dve_chain()], inc=("dve",))
                sq = act(af.ap("ln", 2, 1), af.ap("ln", 1, 1), AF.Sqrt, af.ap("epsv"), 1.0,
                         waits=[dve_chain()])["act"]
                B.emit("dve", lambda e, o=af.ap("ln", 3, 1), i=af.ap("ln", 2, 1):
                       e.reciprocal(o, i), waits=[("act", sq), dve_chain()],
                       inc=("dve",))
                nm = B.emit("dve", lambda e, o=af.ap("ln", 4, 1),
                            i=af.ap("ln", 0, 1), s1=af.ap("ln", 3, 1):
                            e.tensor_scalar(out=o, in0=i, scalar1=s1, scalar2=-1.0,
                                            op0=OP.mult, op1=OP.mult),
                            waits=[dve_chain()], inc=("dve",))["dve"]
                ow = [("dve", nm)]
                if len(out_stores) >= 2:
                    ow.append(out_stores[-2])
                oa = act(af.ap("outb", pr * D, D), af.ap("r2", pr * D, D),
                         AF.Identity, af.ap("ln", 4, 1), af.ap("ln", 3, 1),
                         waits=ow)["act"]
                r2_guard[pr] = oa
                sv2 = d_sp("s_out%d" % pr, out_d[ts * 128:(ts + 1) * 128, :],
                           af.ap("outb", pr * D, D), waits=[("act", oa)])
                out_stores.append(sv2)
            th_barrier_dve = B.val("dve")
            th_barrier_act = B.val("act")

    finals = [(s, B.counts[s]) for s in list(B.counts)
              if s.startswith("s_") and B.counts[s] > 0]
    B.emit("sp", lambda e: e.nop(), waits=finals)
    B.emit("gp", lambda e: e.nop(), waits=finals)

    B.build_block()
    return nc, B



def _host_prep(inputs):
    x = np.ascontiguousarray(inputs["L0_attn_x"][0])
    Wq, bq = inputs["L0_attn_Wq"], inputs["L0_attn_bq"]
    Wk, bk = inputs["L0_attn_Wk"], inputs["L0_attn_bk"]
    Wv, bv = inputs["L0_attn_Wv"], inputs["L0_attn_bv"]
    Wo, bo = inputs["L0_attn_Wo"], inputs["L0_attn_bo"]
    lut = inputs["L0_ffn_lut"]
    g_idx = inputs["L0_ffn_gate_idx"]
    u_idx = inputs["L0_ffn_up_idx"]
    d_idx = inputs["L0_ffn_down_idx"]

    xT = np.ascontiguousarray(x.T)

    freqs = (10000.0 ** (-(np.arange(0, D, 2, dtype=np.float32) / D)))[:HD // 2]
    ang = np.outer(np.arange(T, dtype=np.float32), freqs).astype(np.float32)
    cosf = np.cos(ang).astype(np.float32).T
    sinf = np.sin(ang).astype(np.float32).T
    cosd = np.ascontiguousarray(np.concatenate([cosf, cosf], axis=0))
    sind = np.ascontiguousarray(np.concatenate([sinf, sinf], axis=0))

    kk = np.arange(128)[:, None]
    qq = np.arange(CH)[None, :]
    masks = np.concatenate(
        [((128 * j + kk) <= qq).astype(np.float32) for j in range(4)],
        axis=1).astype(bf16)

    perm = np.concatenate([np.arange(0, HD, 2), np.arange(1, HD, 2)])
    lutb = np.ascontiguousarray(np.tile(lut[None, :], (128, 1)).astype(np.float32))

    g_idxT = np.ascontiguousarray(g_idx.T)
    u_idxT = np.ascontiguousarray(u_idx.T)
    d_idxT = np.ascontiguousarray(d_idx.T)

    in_maps = []
    for c in range(NCORE):
        qh = [2 * c, 2 * c + 1]
        kvh = c // 2
        wq_c = np.empty((D, NH * HD), np.float32)
        bq_c = np.empty((NH * HD,), np.float32)
        for i, h in enumerate(qh):
            wq_c[:, i * HD:(i + 1) * HD] = Wq[:, h * HD + perm]
            bq_c[i * HD:(i + 1) * HD] = bq[h * HD + perm]
        bq_c = np.ascontiguousarray(bq_c.reshape(NH, HD).T)
        wk_c = np.ascontiguousarray(Wk[:, kvh * HD + perm])
        bk_c = np.ascontiguousarray(bk[kvh * HD + perm][:, None])
        wv_c = np.ascontiguousarray(Wv[:, kvh * HD:(kvh + 1) * HD])
        bv_c = np.ascontiguousarray(bv[kvh * HD:(kvh + 1) * HD][:, None])
        x_own_c = np.ascontiguousarray(
            np.concatenate([x[c * CH:(c + 1) * CH],
                            x[(8 + c) * CH:(9 + c) * CH]], axis=0))
        rg = slice(c * (D // NCORE), (c + 1) * (D // NCORE))
        rd = slice(c * (INTER // NCORE), (c + 1) * (INTER // NCORE))
        in_maps.append({
            "xT": xT, "x_own": x_own_c,
            "wq": np.ascontiguousarray(wq_c), "bq": bq_c,
            "wk": wk_c, "bk": bk_c, "wv": wv_c, "bv": bv_c,
            "wo": Wo, "bo": np.ascontiguousarray(bo[None, :]),
            "cosd": cosd, "sind": sind, "maskd": masks, "lutd": lutb,
            "gidx": np.ascontiguousarray(g_idxT[rg]).astype(bf16),
            "uidx": np.ascontiguousarray(u_idxT[rg]).astype(bf16),
            "didx": np.ascontiguousarray(d_idxT[rd]).astype(bf16),
        })
    return in_maps


def kernel(**inputs):
    in_maps = _host_prep(inputs)
    nc, _ = build_nc()
    res = run_bass_kernel_spmd(nc, in_maps, core_ids=list(range(NCORE)))
    out = np.empty((T, D), np.float32)
    for c in range(NCORE):
        oc = res.results[c]["out"]
        out[c * CH:(c + 1) * CH] = oc[:CH]
        out[(8 + c) * CH:(9 + c) * CH] = oc[CH:]
    return out[None, :, :]




# version 09
# speedup vs baseline: 1.1118x; 1.1118x over previous
import numpy as np
import ml_dtypes
import contextlib

import concourse.bass as bass
import concourse.mybir as mybir
from concourse.bass_utils import run_bass_kernel_spmd
from concourse.alu_op_type import AluOpType
from concourse.aot_env import allow_ici_all_to_all

F32 = mybir.dt.float32
BF16 = mybir.dt.bfloat16
AF = mybir.ActivationFunctionType
OP = AluOpType
bf16 = ml_dtypes.bfloat16

NCORE = 8
T = 8192
D = 2048
HD = 128
NH = 2
CH = 512
NCH = T // CH
INTER = 5504
NIT = INTER // 128
NDM = D // 128
TOK = 2 * CH
NTS = TOK // 128
EPS = 1e-5
OSCALE = float(D) ** -0.5
C_SHIFT = 0.0

ENGS = ("pe", "act", "dve", "gp", "sp")


class Builder:
    def __init__(self, nc):
        self.nc = nc
        self.ops = {e: [] for e in ENGS}
        self.sems = {}
        self.counts = {}
        self.es = contextlib.ExitStack()

    def sem(self, name):
        if name not in self.sems:
            self.sems[name] = self.es.enter_context(self.nc.semaphore(name))
            self.counts[name] = 0
        return self.sems[name]

    def val(self, name):
        return self.counts.get(name, 0)

    def emit(self, eng, fn, waits=(), inc=(), dma=False):
        step = 16 if dma else 1
        out_vals = {}
        incs = []
        for s in inc:
            self.sem(s)
            self.counts[s] += step
            out_vals[s] = self.counts[s]
            incs.append((self.sems[s], step))
        merged = {}
        for (s, v) in waits:
            if v > 0:
                merged[s] = max(merged.get(s, 0), v)
        wl = [(self.sem(s), v) for s, v in merged.items()]
        self.ops[eng].append((wl, fn, incs))
        return out_vals

    def build_block(self):
        ops = self.ops
        with self.nc.Block() as block:
            def runner(eng_name):
                def _run(eng):
                    for wl, fn, incs in ops[eng_name]:
                        for sem, v in wl:
                            eng.wait_ge(sem, v)
                        instr = fn(eng)
                        for sem, step in incs:
                            if instr is None:
                                instr = eng.nop()
                            instr = instr.then_inc(sem, step)
                return _run
            block.tensor(runner("pe"))
            block.scalar(runner("act"))
            block.vector(runner("dve"))
            block.gpsimd(runner("gp"))
            block.sync(runner("sp"))


class Arena:

    def __init__(self, tensor, ncols, align=16):
        self.t = tensor
        self.n = ncols
        self.align = align
        self.live = {}

    def alloc(self, name, cols):
        assert name not in self.live
        al = self.align
        taken = sorted(self.live.values())
        start = 0
        for s, ln in taken:
            if start + cols <= s:
                break
            start = -(-(s + ln) // al) * al
        assert start + cols <= self.n, (
            f"arena overflow: {name} needs {cols} @ {start}, cap {self.n}")
        self.live[name] = (start, cols)
        return start

    def free(self, *names):
        for nm in names:
            del self.live[nm]

    def ap(self, name, c0=0, cols=None, p0=0, pn=128):
        s, ln = self.live[name]
        if cols is None:
            cols = ln - c0
        assert c0 + cols <= ln, (name, c0, cols, ln)
        return self.t[p0:p0 + pn, s + c0:s + c0 + cols]


def build_nc():
    nc = bass.Bass("TRN2", target_bir_lowering=False)

    def din(name, shape, dt=F32):
        return nc.dram_tensor(name, shape, dt, kind="ExternalInput").ap()

    xT = din("xT", [D, T])
    x_own = din("x_own", [TOK, D])
    wq = din("wq", [D, NH * HD])
    wk = din("wk", [D, HD])
    wv = din("wv", [D, HD])
    wo = din("wo", [D, D])
    bq = din("bq", [128, NH])
    bk = din("bk", [HD, 1])
    bv = din("bv", [HD, 1])
    bo = din("bo", [1, D])
    cosd = din("cosd", [128, T])
    sind = din("sind", [128, T])
    maskd = din("maskd", [128, 4 * CH], BF16)
    lutd = din("lutd", [128, 16])
    gidx = din("gidx", [D // NCORE, INTER], BF16)
    uidx = din("uidx", [D // NCORE, INTER], BF16)
    didx = din("didx", [INTER // NCORE, D], BF16)
    out_d = nc.dram_tensor("out", [TOK, D], F32, kind="ExternalOutput").ap()

    ag_g_in = nc.dram_tensor("ag_g_in", [D // NCORE, INTER], BF16).ap()
    ag_u_in = nc.dram_tensor("ag_u_in", [D // NCORE, INTER], BF16).ap()
    ag_d_in = nc.dram_tensor("ag_d_in", [INTER // NCORE, D], BF16).ap()
    ag_g = nc.dram_tensor("ag_g", [D, INTER], BF16, addr_space="Shared").ap()
    ag_u = nc.dram_tensor("ag_u", [D, INTER], BF16, addr_space="Shared").ap()
    ag_d = nc.dram_tensor("ag_d", [INTER, D], BF16, addr_space="Shared").ap()
    a2a_in = [nc.dram_tensor(f"a2a_in{i}", [NCORE * NH * HD, CH], BF16).ap()
              for i in range(2)]
    a2a_out = [nc.dram_tensor(f"a2a_out{i}", [NCORE * NH * HD, CH], BF16).ap()
               for i in range(2)]

    B = Builder(nc)
    es = B.es

    BF_COLS = 81800
    F32_COLS = 12000
    abf = Arena(es.enter_context(nc.sbuf_tensor("abf", [128, BF_COLS], BF16)),
                BF_COLS, align=16)
    af = Arena(es.enter_context(nc.sbuf_tensor("af32", [128, F32_COLS], F32)),
               F32_COLS, align=8)
    pb = [es.enter_context(nc.psum_tensor(f"pb{i}", [128, 512], F32))
          for i in range(8)]

    af.alloc("lut", 16)
    af.alloc("bq", NH)
    af.alloc("bk", 1)
    af.alloc("bv", 1)
    af.alloc("bo_bc", D)
    af.alloc("sonesf", 128)
    af.alloc("ln", 16)
    af.alloc("epsv", 1)
    af.alloc("bn6", 24)
    abf.alloc("masks", 4 * CH)
    abf.alloc("sones", 1)

    def d_sp(sem, dst, src, waits=()):
        v = B.emit("sp", lambda e, dst=dst, src=src: e.dma_start(out=dst, in_=src),
                   waits=waits, inc=(sem,), dma=True)
        return (sem, v[sem])

    def d_gp(sem, dst, src, waits=()):
        v = B.emit("gp", lambda e, dst=dst, src=src: e.dma_start(out=dst, in_=src),
                   waits=waits, inc=(sem,), dma=True)
        return (sem, v[sem])

    def tsp(sem, dst, src, waits=()):
        v = B.emit("sp", lambda e, dst=dst, src=src: e.dma_start(
            out=dst, in_=src, transpose=True), waits=waits, inc=(sem,), dma=True)
        return (sem, v[sem])

    def mm(out, lhsT, rhs, start, stop, waits=(), inc=()):
        return B.emit("pe", lambda e, out=out, lhsT=lhsT, rhs=rhs, start=start,
                      stop=stop: e.matmul(out, lhsT, rhs, start=start, stop=stop),
                      waits=waits, inc=inc)

    def act(outp, inp, func, bias, scale, waits=()):
        return B.emit("act", lambda e, outp=outp, inp=inp, func=func, bias=bias,
                      scale=scale: e.activation(outp, inp, func, bias=bias,
                                                scale=scale),
                      waits=waits, inc=("act",))

    def tt(outp, in0, in1, op, waits=()):
        return B.emit("dve", lambda e, outp=outp, in0=in0, in1=in1, op=op:
                      e.tensor_tensor(out=outp, in0=in0, in1=in1, op=op),
                      waits=waits, inc=("dve",))

    def dve_chain():
        return ("dve", B.val("dve"))

    d_sp("s_ld", af.ap("lut"), lutd)
    d_sp("s_ld", af.ap("bq"), bq)
    d_sp("s_ld", af.ap("bk"), bk)
    d_sp("s_ld", af.ap("bv"), bv)
    d_sp("s_ld", abf.ap("masks"), maskd)
    ld0 = d_sp("s_ld", af.ap("bo_bc", 0, D, pn=1), bo)

    abf.alloc("xbuf", 2 * NDM * CH)
    abf.alloc("wq", NDM * NH * HD)
    abf.alloc("wk", NDM * HD)
    abf.alloc("wv", NDM * HD)
    af.alloc("cc", 2 * CH)
    af.alloc("ss", 2 * CH)
    af.alloc("ktmp", 2 * CH)
    af.alloc("qtmp", 4 * CH)
    af.alloc("ra", 2 * CH)
    af.alloc("rb", 2 * CH)
    abf.alloc("vbfT", 2 * CH)
    abf.alloc("kT", T)
    abf.alloc("qT", NH * T)
    abf.alloc("v", T)
    abf.alloc("p", 4 * CH)
    abf.alloc("o", 2 * CH)
    af.alloc("lrow", CH)
    af.alloc("lrec", CH)
    af.alloc("bc", 2 * CH)
    abf.alloc("idx", INTER)
    abf.alloc("wacc", INTER)
    abf.alloc("wcon", INTER)

    d_gp("s_ldw", abf.ap("wq"), wq.rearrange("(n p) m -> p n m", p=128))
    d_gp("s_ldw", abf.ap("wk"), wk.rearrange("(n p) m -> p n m", p=128))
    ldw = d_gp("s_ldw", abf.ap("wv"), wv.rearrange("(n p) m -> p n m", p=128))
    B.emit("dve", lambda e, a=abf.ap("sones"): e.memset(a, 1.0), inc=("dve",))
    B.emit("dve", lambda e, a=af.ap("epsv"): e.memset(a, EPS), inc=("dve",))
    ones_init = B.emit("dve", lambda e, a=af.ap("sonesf", pn=1): e.memset(a, OSCALE),
                       inc=("dve",))["dve"]

    def wtiles(nrows):
        tl, r = [], 0
        while r < nrows:
            tl.append((r, min(128, nrows - r)))
            r += 128
        return tl

    wb_done = {}
    gather_tiles = []
    for mat, idx_d, ag_in_d, fd in (("g", gidx, ag_g_in, INTER),
                                    ("u", uidx, ag_u_in, INTER),
                                    ("d", didx, ag_d_in, D)):
        tl = wtiles(idx_d.shape[0])
        for ti, (r0, nr) in enumerate(tl):
            gather_tiles.append((mat, idx_d, ag_in_d, fd, r0, nr,
                                 ti == len(tl) - 1))

    _gstate = {"prev_store": None, "emitted": 0}

    def emit_gather_tile(k):
        mat, idx_d, ag_in_d, fd, r0, nr, last = gather_tiles[k]
        idx_ap = abf.ap("idx", 0, fd, pn=nr)
        acc_ap = abf.ap("wacc", 0, fd, pn=nr)
        con_ap = abf.ap("wcon", 0, fd, pn=nr)
        ps = _gstate["prev_store"]
        lv = d_gp("s_idx", idx_ap, idx_d[r0:r0 + nr, :],
                  waits=[ps] if ps else [])
        B.emit("dve", lambda e, o=acc_ap, i=idx_ap, s2=af.ap("lut", 0, 1, pn=nr):
               e.tensor_scalar(out=o, in0=i, scalar1=0.0, scalar2=s2,
                               op0=OP.is_equal, op1=OP.mult),
               waits=[lv, ld0, dve_chain()], inc=("dve",))
        for j in range(1, 16):
            B.emit("dve", lambda e, o=con_ap, i=idx_ap, j=j,
                   s2=af.ap("lut", j, 1, pn=nr):
                   e.tensor_scalar(out=o, in0=i, scalar1=float(j), scalar2=s2,
                                   op0=OP.is_equal, op1=OP.mult),
                   waits=[dve_chain()], inc=("dve",))
            tt(acc_ap, acc_ap, con_ap, OP.add, waits=[dve_chain()])
        _gstate["prev_store"] = d_gp("s_wst_" + mat, ag_in_d[r0:r0 + nr, :],
                                     acc_ap, waits=[dve_chain()])
        if last:
            wb_done[mat] = _gstate["prev_store"]

    x_dma = {}
    pe_xfree = {}
    act_evac = {}
    kq_done = {}
    vT_done = {}

    for tc in range(NCH):
        par = tc % 2
        w = [("pe", pe_xfree[tc - 2])] if tc >= 2 else []
        x_dma[tc] = d_gp("s_x%d" % par, abf.ap("xbuf", par * NDM * CH, NDM * CH),
                         xT[:, tc * CH:(tc + 1) * CH].rearrange(
                             "(n p) m -> p n m", p=128), waits=w)
        csw = [("dve", kq_done[tc - 2])] if tc >= 2 else []
        d_sp("s_cs%d" % par, af.ap("cc", par * CH, CH),
             cosd[:, tc * CH:(tc + 1) * CH], waits=csw)
        cs_val = d_sp("s_cs%d" % par, af.ap("ss", par * CH, CH),
                      sind[:, tc * CH:(tc + 1) * CH])

        pew = [x_dma[tc], ldw]
        if tc >= 2:
            pew += [("act", act_evac[("k", tc - 2)]),
                    ("act", act_evac[("v", tc - 2)])]
        if tc >= 1:
            pew.append(("act", act_evac[("q1", tc - 1)]))
        for dm in range(NDM):
            iv = mm(pb[par][:, :], abf.ap("wk", dm * HD, HD),
                    abf.ap("xbuf", par * NDM * CH + dm * CH, CH),
                    start=(dm == 0), stop=(dm == NDM - 1),
                    waits=pew if dm == 0 else (),
                    inc=("pe",) if dm == NDM - 1 else ())
        pe_k = iv["pe"]
        for dm in range(NDM):
            iv = mm(pb[2 + par][:, :], abf.ap("wv", dm * HD, HD),
                    abf.ap("xbuf", par * NDM * CH + dm * CH, CH),
                    start=(dm == 0), stop=(dm == NDM - 1),
                    inc=("pe",) if dm == NDM - 1 else ())
        pe_v = iv["pe"]
        pe_q = {}
        for h in range(NH):
            for dm in range(NDM):
                iv = mm(pb[4 + h][:, :], abf.ap("wq", dm * NH * HD + h * HD, HD),
                        abf.ap("xbuf", par * NDM * CH + dm * CH, CH),
                        start=(dm == 0), stop=(dm == NDM - 1),
                        inc=("pe",) if dm == NDM - 1 else ())
            pe_q[h] = iv["pe"]
        pe_xfree[tc] = pe_q[NH - 1]

        roped_prev = kq_done.get(tc - 1, 0)
        act(af.ap("ktmp", 0, CH, p0=0, pn=64), pb[par][0:64, :], AF.Identity,
            af.ap("bk", p0=0, pn=64), 1.0,
            waits=[("pe", pe_k), ("dve", roped_prev), ld0])
        act_evac[("k", tc)] = act(
            af.ap("ktmp", CH, CH, p0=0, pn=64), pb[par][64:128, :], AF.Identity,
            af.ap("bk", p0=64, pn=64), 1.0)["act"]
        vw = [("pe", pe_v)]
        if tc >= 2:
            vw.append(vT_done[tc - 2])
        act_evac[("v", tc)] = act(abf.ap("vbfT", par * CH, CH), pb[2 + par][:, :],
                                  AF.Identity, af.ap("bv"), 1.0, waits=vw)["act"]
        for h in range(NH):
            act(af.ap("qtmp", h * 2 * CH, CH, p0=0, pn=64), pb[4 + h][0:64, :],
                AF.Identity, af.ap("bq", h, 1, p0=0, pn=64), 1.0,
                waits=[("pe", pe_q[h]), ("dve", roped_prev)])
            act_evac[(f"q{h}", tc)] = act(
                af.ap("qtmp", h * 2 * CH + CH, CH, p0=0, pn=64),
                pb[4 + h][64:128, :], AF.Identity,
                af.ap("bq", h, 1, p0=64, pn=64), 1.0)["act"]

        for j in range(CH // 128):
            vts = tsp("s_vt%d" % par, abf.ap("v", tc * CH + j * 128, 128),
                      abf.ap("vbfT", par * CH + j * 128, 128),
                      waits=[("act", act_evac[("v", tc)])] if j == 0 else ())
        vT_done[tc] = vts

        def rope(lo_ap, hi_ap, dst_name, dst_off, wait_act):
            t1 = af.ap("ra", par * CH, CH, pn=64)
            t2 = af.ap("rb", par * CH, CH, pn=64)
            cc64 = af.ap("cc", par * CH, CH, pn=64)
            ss64 = af.ap("ss", par * CH, CH, pn=64)
            tt(t1, lo_ap, cc64, OP.mult,
               waits=[("act", wait_act), cs_val, dve_chain()])
            tt(t2, hi_ap, ss64, OP.mult, waits=[dve_chain()])
            tt(abf.ap(dst_name, dst_off, CH, p0=0, pn=64), t1, t2, OP.subtract,
               waits=[dve_chain()])
            tt(t1, lo_ap, ss64, OP.mult, waits=[dve_chain()])
            tt(t2, hi_ap, cc64, OP.mult, waits=[dve_chain()])
            tt(abf.ap(dst_name, dst_off, CH, p0=64, pn=64), t1, t2, OP.add,
               waits=[dve_chain()])

        rope(af.ap("ktmp", 0, CH, pn=64), af.ap("ktmp", CH, CH, pn=64),
             "kT", tc * CH, act_evac[("k", tc)])
        for h in range(NH):
            rope(af.ap("qtmp", h * 2 * CH, CH, pn=64),
                 af.ap("qtmp", h * 2 * CH + CH, CH, pn=64),
                 "qT", h * T + tc * CH, act_evac[(f"q{h}", tc)])
        kq_done[tc] = B.val("dve")

    proj_done_dve = B.val("dve")
    vt_final = [("s_vt0", B.val("s_vt0")), ("s_vt1", B.val("s_vt1"))]

    bo_w = [("pe", pe_xfree[NCH - 1]), ("act", act_evac[("q1", NCH - 1)]),
            ("dve", ones_init), ld0]
    bo_bc_done = 0
    for c4 in range(4):
        pv = mm(pb[4][:, :], af.ap("sonesf", 0, 128, pn=1),
                af.ap("bo_bc", c4 * 512, 512, pn=1), True, True,
                waits=bo_w if c4 == 0 else [("dve", bo_bc_done)],
                inc=("pe",))["pe"]
        bo_bc_done = B.emit("dve", lambda e, o=af.ap("bo_bc", c4 * 512, 512),
                            i=pb[c4 * 0 + 4][:, :]:
                            e.tensor_scalar(out=o, in0=i, scalar1=float(D) ** 0.5,
                                            scalar2=None, op0=OP.mult,
                                            op1=OP.bypass),
                            waits=[("pe", pv), dve_chain()], inc=("dve",))["dve"]

    NKT = CH // 128
    o_done = {}
    pslot_free = {}
    bank_free_qk = {}
    pv_free = {}
    bc_free = {}
    l_free = 0
    bv_prev = 0
    bc_prev = 0
    o_store_seq = {}
    attn_barrier = [("dve", max(proj_done_dve, bo_bc_done)), ld0] + vt_final
    qk_count = 0
    seq = 0
    exp_prev_group = 0
    avl_prev = 0
    for qc in range(NCH):
        for h in range(NH):
            hp = seq % 2
            ng = qc + 1
            for g in range(ng):
                diag = (g == ng - 1)
                w = list(attn_barrier) if qk_count == 0 else []
                if exp_prev_group:
                    w.append(("act", exp_prev_group))
                for j in range(4):
                    kt = g * 4 + j
                    qv = mm(pb[j][:, :], abf.ap("kT", kt * 128, 128),
                            abf.ap("qT", h * T + qc * CH, CH), True, True,
                            waits=w if j == 0 else (),
                            inc=("pe",) if j == 3 else ())
                qk_g = qv["pe"]
                ew = [("pe", qk_g)]
                if avl_prev:
                    ew.append(("pe", avl_prev))
                for j in range(4):
                    av = B.emit("act", lambda e, o=abf.ap("p", j * CH, CH),
                                i=pb[j][:, :]:
                                e.activation(o, i, AF.Exp, bias=-C_SHIFT,
                                             scale=1.0),
                                waits=ew if j == 0 else (), inc=("act",))["act"]
                exp_prev_group = av
                pdep = ("act", av)
                if diag:
                    for j in range(4):
                        mv = tt(abf.ap("p", j * CH, CH), abf.ap("p", j * CH, CH),
                                abf.ap("masks", j * CH, CH), OP.mult,
                                waits=[("act", av), dve_chain()] if j == 0
                                else [dve_chain()])["dve"]
                    pdep = ("dve", mv)
                for j in range(4):
                    kt = g * 4 + j
                    avw = []
                    if j == 0:
                        avw.append(pdep)
                        if kt == 0 and hp in pv_free:
                            avw.append(("dve", pv_free[hp]))
                    mm(pb[6 + hp][:, :], abf.ap("v", kt * 128, 128),
                       abf.ap("p", j * CH, CH),
                       start=(kt == 0), stop=(kt == 4 * ng - 1), waits=avw)
                    lw = [("dve", l_free)] if (kt == 0 and l_free) else []
                    lvv = mm(pb[5][0:1, :], abf.ap("sones"),
                             abf.ap("p", j * CH, CH),
                             start=(kt == 0), stop=(kt == 4 * ng - 1), waits=lw,
                             inc=("pe",) if j == 3 else ())
                    if j == 3:
                        lv = lvv["pe"]
                avl_prev = lv
                qk_count += 4
            B.emit("dve", lambda e, o=af.ap("lrow", 0, CH, pn=1),
                   i=pb[5][0:1, :]: e.tensor_copy(o, i),
                   waits=[("pe", lv), dve_chain()], inc=("dve",))
            l_free = B.emit("dve", lambda e, o=af.ap("lrec", 0, CH, pn=1),
                            i=af.ap("lrow", 0, CH, pn=1): e.reciprocal(o, i),
                            waits=[dve_chain(), ("pe", bv_prev)], inc=("dve",))["dve"]
            bw = [("dve", l_free), ("dve", bc_prev)]
            bv_ = mm(pb[4][:, :], af.ap("sonesf", 0, 128, pn=1),
                     af.ap("lrec", 0, CH, pn=1), True, True, waits=bw,
                     inc=("pe",))["pe"]
            bv_prev = bv_
            bc_free[hp] = B.emit("dve", lambda e, o=af.ap("bc", hp * CH, CH),
                                 i=pb[4][:, :]: e.tensor_copy(o, i),
                                 waits=[("pe", bv_), dve_chain()],
                                 inc=("dve",))["dve"]
            bc_prev = bc_free[hp]
            ow = [("dve", bc_free[hp])]
            if seq >= 2:
                ow.append(o_store_seq[seq - 2])
            om = tt(abf.ap("o", hp * CH, CH), pb[6 + hp][:, :],
                    af.ap("bc", hp * CH, CH), OP.mult, waits=ow)["dve"]
            pv_free[hp] = om
            half = qc // (NCH // 2)
            dest = qc % (NCH // 2)
            ov = d_sp("s_o%d%d" % (half, hp),
                      a2a_in[half][dest * NH * HD + h * HD:
                                   dest * NH * HD + (h + 1) * HD, :],
                      abf.ap("o", hp * CH, CH), waits=[("dve", om)])
            o_done[(h, qc)] = ov
            o_store_seq[seq] = ov
            ngt = len(gather_tiles)
            want = min(ngt, (seq + 1) * ngt // 24)
            while _gstate["emitted"] < want:
                emit_gather_tile(_gstate["emitted"])
                _gstate["emitted"] += 1
            seq += 1

    while _gstate["emitted"] < len(gather_tiles):
        emit_gather_tile(_gstate["emitted"])
        _gstate["emitted"] += 1

    cc_vals = {}

    def collective(kind, ins, outs, waits):
        def fn(e, kind=kind, ins=ins, outs=outs):
            if kind == "AllToAll":
                with allow_ici_all_to_all():
                    return e.collective_compute(
                        kind, OP.bypass, replica_groups=[list(range(NCORE))],
                        ins=ins, outs=outs)
            return e.collective_compute(
                kind, OP.bypass, replica_groups=[list(range(NCORE))],
                ins=ins, outs=outs)
        return B.emit("gp", fn, waits=waits, inc=("cc",))["cc"]

    cc_vals["g"] = collective("AllGather", [ag_g_in.opt()], [ag_g.opt()],
                              [wb_done["g"]])
    cc_vals["u"] = collective("AllGather", [ag_u_in.opt()], [ag_u.opt()],
                              [wb_done["u"]])
    cc_vals["a1"] = collective(
        "AllToAll", [a2a_in[0].opt()], [a2a_out[0].opt()],
        [("s_o00", B.val("s_o00")), ("s_o01", B.val("s_o01"))])
    cc_vals["a2"] = collective(
        "AllToAll", [a2a_in[1].opt()], [a2a_out[1].opt()],
        [("s_o10", B.val("s_o10")), ("s_o11", B.val("s_o11"))])
    cc_vals["d"] = collective("AllGather", [ag_d_in.opt()], [ag_d.opt()],
                              [wb_done["d"]])

    abf.free("xbuf", "wq", "wk", "wv", "vbfT", "kT", "qT", "v", "p", "o",
             "idx", "wacc", "wcon")
    af.free("cc", "ss", "ktmp", "qtmp", "ra", "rb", "lrow", "lrec", "bc")

    abf.alloc("oown", NDM * TOK)
    abf.alloc("woslab", 2 * D)
    abf.alloc("x1bf", NTS * D)
    abf.alloc("x1T", NDM * TOK)
    af.alloc("xbo", 2 * D)
    af.alloc("r1", 2 * D)

    for dt in range(NDM):
        for half in range(2):
            d_sp("s_ol", abf.ap("oown", dt * TOK + half * CH, CH),
                 a2a_out[half][dt * 128:(dt + 1) * 128, :],
                 waits=[("cc", cc_vals["a1" if half == 0 else "a2"])])
    oload_all = ("s_ol", B.val("s_ol"))

    NOROUND = NTS // 2
    wo_pe = {}
    evac_done = {}
    x1_done = {}
    xbo_add = {}

    for rnd in range(NOROUND):
        ts0, ts1 = 2 * rnd, 2 * rnd + 1
        for ts in (ts0, ts1):
            pr = ts % 2
            w = []
            if ts >= 2:
                w.append(("dve", evac_done[(rnd - 1, (ts % 2) * 4 + 3)]))
            xv = d_sp("s_xo%d" % pr, af.ap("xbo", pr * D, D),
                      x_own[ts * 128:(ts + 1) * 128, :], waits=w)
            xbo_add[ts] = tt(af.ap("xbo", pr * D, D), af.ap("xbo", pr * D, D),
                             af.ap("bo_bc"), OP.add,
                             waits=[xv, ("dve", max(B.val("dve"),
                                                    bo_bc_done))])["dve"]
        for dt in range(NDM):
            par = dt % 2
            w = []
            key_prev = (rnd, dt - 2) if dt >= 2 else (rnd - 1, dt + NDM - 2)
            if key_prev in wo_pe:
                w.append(("pe", wo_pe[key_prev]))
            gval = d_gp("s_wo%d" % par, abf.ap("woslab", par * D, D),
                        wo[dt * 128:(dt + 1) * 128, :], waits=w)
            for i, ts in enumerate((ts0, ts1)):
                mw = [gval]
                if dt == 0:
                    mw.append(oload_all)
                    for c4 in range(4):
                        if (rnd - 1, i * 4 + c4) in evac_done:
                            mw.append(("dve", evac_done[(rnd - 1, i * 4 + c4)]))
                for c4 in range(4):
                    iv = mm(pb[i * 4 + c4][:, :],
                            abf.ap("oown", dt * TOK + ts * 128, 128),
                            abf.ap("woslab", par * D + c4 * 512, 512),
                            start=(dt == 0), stop=(dt == NDM - 1),
                            waits=mw if c4 == 0 else (),
                            inc=("pe",) if (c4 == 3 and i == 1) else ())
            wo_pe[(rnd, dt)] = iv["pe"]
        mm_done = wo_pe[(rnd, NDM - 1)]
        for i, ts in enumerate((ts0, ts1)):
            pr = ts % 2
            for c4 in range(4):
                ev = tt(af.ap("r1", pr * D + c4 * 512, 512), pb[i * 4 + c4][:, :],
                        af.ap("xbo", pr * D + c4 * 512, 512), OP.add,
                        waits=[("pe", mm_done),
                               ("act", x1_done.get(ts - 2, 0)),
                               ("dve", max(B.val("dve"), xbo_add[ts]))])["dve"]
                evac_done[(rnd, i * 4 + c4)] = ev
            dvw = B.val("dve")
            for c4 in range(4):
                B.emit("dve", lambda e, o=af.ap("bn6", c4 * 6, 6),
                       i=af.ap("r1", pr * D + c4 * 512, 512): e.bn_stats(o, i),
                       waits=[("dve", dvw)] if c4 == 0 else (), inc=("dve",))
            B.emit("dve", lambda e, o=af.ap("ln", 0, 2), i=af.ap("bn6"):
                   e.bn_aggr(o, i), waits=[dve_chain()], inc=("dve",))
            sq = act(af.ap("ln", 2, 1), af.ap("ln", 1, 1), AF.Sqrt, af.ap("epsv"), 1.0,
                     waits=[dve_chain()])["act"]
            B.emit("dve", lambda e, o=af.ap("ln", 3, 1), i=af.ap("ln", 2, 1):
                   e.reciprocal(o, i), waits=[("act", sq), dve_chain()],
                   inc=("dve",))
            nm = B.emit("dve", lambda e, o=af.ap("ln", 4, 1), i=af.ap("ln", 0, 1),
                        s1=af.ap("ln", 3, 1):
                        e.tensor_scalar(out=o, in0=i, scalar1=s1, scalar2=-1.0,
                                        op0=OP.mult, op1=OP.mult),
                        waits=[dve_chain()], inc=("dve",))["dve"]
            x1_done[ts] = act(abf.ap("x1bf", ts * D, D), af.ap("r1", pr * D, D),
                              AF.Identity, af.ap("ln", 4, 1), af.ap("ln", 3, 1),
                              waits=[("dve", nm)])["act"]
            for dt in range(NDM):
                tv = tsp("s_x1t", abf.ap("x1T", dt * TOK + ts * 128, 128),
                         abf.ap("x1bf", ts * D + dt * 128, 128),
                         waits=[("act", x1_done[ts])] if dt == 0 else ())
    x1T_all = ("s_x1t", B.val("s_x1t"))

    abf.free("oown", "woslab")
    af.free("xbo", "r1")
    abf.alloc("hT", NIT * CH)
    abf.alloc("wgslab", 2 * D)
    abf.alloc("wuslab", 2 * D)
    abf.alloc("wdslab", 2 * D)
    af.alloc("sg", 2 * 512)
    af.alloc("r2", 2 * D)
    af.alloc("outb", 2 * D)

    out_stores = []
    r2_guard = {0: 0, 1: 0}
    gslab_pe = {}
    prev_down_pe = 0
    th_barrier_dve = 0
    th_barrier_act = 0
    for th in range(2):
        silu_done = {}
        hmul_done = {}
        for it in range(NIT):
            par = it % 2
            w = []
            if (th, it - 2) in gslab_pe:
                w.append(("pe", gslab_pe[(th, it - 2)]))
            elif (th - 1, NIT - 2 + it) in gslab_pe:
                w.append(("pe", gslab_pe[(th - 1, NIT - 2 + it)]))
            if th == 1 and it < 2:
                w.append(("pe", prev_down_pe))
            ccw = [("cc", cc_vals["g"])] if (th == 0 and it < 2) else []
            d_sp("s_wgu%d" % par, abf.ap("wgslab", par * D, D),
                 ag_g[:, it * 128:(it + 1) * 128].rearrange(
                     "(n p) m -> p n m", p=128), waits=w + ccw)
            ccw = [("cc", cc_vals["u"])] if (th == 0 and it < 2) else []
            slab_val = d_sp("s_wgu%d" % par, abf.ap("wuslab", par * D, D),
                            ag_u[:, it * 128:(it + 1) * 128].rearrange(
                                "(n p) m -> p n m", p=128), waits=ccw)
            mw = [slab_val, x1T_all]
            if th == 0 and it == 0:
                mw += [("dve", B.val("dve")), ("act", B.val("act"))]
            if th == 1 and it < 2:
                mw += [("dve", th_barrier_dve), ("act", th_barrier_act)]
            if (th, it - 2) in silu_done:
                mw += [("act", silu_done[(th, it - 2)]),
                       ("dve", hmul_done[(th, it - 2)])]
            for dm in range(NDM):
                mm(pb[par][:, :], abf.ap("wgslab", par * D + dm * 128, 128),
                   abf.ap("x1T", dm * TOK + th * CH, CH),
                   start=(dm == 0), stop=(dm == NDM - 1),
                   waits=mw if dm == 0 else ())
            for dm in range(NDM):
                iv = mm(pb[2 + par][:, :], abf.ap("wuslab", par * D + dm * 128, 128),
                        abf.ap("x1T", dm * TOK + th * CH, CH),
                        start=(dm == 0), stop=(dm == NDM - 1),
                        inc=("pe",) if dm == NDM - 1 else ())
            gmv = iv["pe"]
            gslab_pe[(th, it)] = gmv
            sw = [("pe", gmv)]
            if (th, it - 2) in hmul_done:
                sw.append(("dve", hmul_done[(th, it - 2)]))
            elif th == 1 and it < 2:
                sw.append(("dve", th_barrier_dve))
            silu_done[(th, it)] = act(af.ap("sg", par * 512, 512), pb[par][:, :],
                                      AF.Silu, 0.0, 1.0, waits=sw)["act"]
            hmul_done[(th, it)] = B.emit(
                "dve", lambda e, o=abf.ap("hT", it * CH, CH),
                i0=af.ap("sg", par * 512, 512), i1=pb[2 + par][:, :]:
                e.tensor_tensor(out=o, in0=i0, in1=i1, op=OP.mult),
                waits=[("act", silu_done[(th, it)]), dve_chain()],
                inc=("dve",))["dve"]
        hT_ready = max(hmul_done.values())
        last_silu = max(silu_done.values())

        dpair_pe = {}
        for pr2 in range(2):
            ts2a, ts2b = 2 * pr2, 2 * pr2 + 1
            for it in range(NIT):
                par = it % 2
                w = []
                if (pr2, it - 2) in dpair_pe:
                    w.append(("pe", dpair_pe[(pr2, it - 2)]))
                elif it < 2 and prev_down_pe:
                    w.append(("pe", prev_down_pe))
                ccw = ([("cc", cc_vals["d"])]
                       if (th == 0 and pr2 == 0 and it < 2) else [])
                sval = d_sp("s_wd%d" % par, abf.ap("wdslab", par * D, D),
                            ag_d[it * 128:(it + 1) * 128, :],
                            waits=w + ccw)
                mw = [sval, ("dve", hT_ready)]
                if it == 0:
                    mw += [("dve", B.val("dve")), ("act", B.val("act"))]
                for i2, ts2 in enumerate((ts2a, ts2b)):
                    for hc in range(4):
                        iv = mm(pb[i2 * 4 + hc][:, :],
                                abf.ap("hT", it * CH + ts2 * 128, 128),
                                abf.ap("wdslab", par * D + hc * 512, 512),
                                start=(it == 0), stop=(it == NIT - 1),
                                waits=mw if (i2 == 0 and hc == 0) else (),
                                inc=("pe",) if (i2 == 1 and hc == 3) else ())
                dpair_pe[(pr2, it)] = iv["pe"]
            down_done = dpair_pe[(pr2, NIT - 1)]
            prev_down_pe = down_done
            for i2, ts2 in enumerate((ts2a, ts2b)):
                ts = th * 4 + ts2
                pr = i2
                for hc in range(4):
                    tt(af.ap("r2", pr * D + hc * 512, 512), pb[i2 * 4 + hc][:, :],
                       abf.ap("x1bf", ts * D + hc * 512, 512), OP.add,
                       waits=[("pe", down_done), dve_chain(),
                              ("act", r2_guard[pr])])
                dvv = B.val("dve")
                for c4 in range(4):
                    B.emit("dve", lambda e, o=af.ap("bn6", c4 * 6, 6),
                           i=af.ap("r2", pr * D + c4 * 512, 512): e.bn_stats(o, i),
                           waits=[("dve", dvv)] if c4 == 0 else (), inc=("dve",))
                B.emit("dve", lambda e, o=af.ap("ln", 0, 2), i=af.ap("bn6"):
                       e.bn_aggr(o, i), waits=[dve_chain()], inc=("dve",))
                sq = act(af.ap("ln", 2, 1), af.ap("ln", 1, 1), AF.Sqrt, af.ap("epsv"), 1.0,
                         waits=[dve_chain()])["act"]
                B.emit("dve", lambda e, o=af.ap("ln", 3, 1), i=af.ap("ln", 2, 1):
                       e.reciprocal(o, i), waits=[("act", sq), dve_chain()],
                       inc=("dve",))
                nm = B.emit("dve", lambda e, o=af.ap("ln", 4, 1),
                            i=af.ap("ln", 0, 1), s1=af.ap("ln", 3, 1):
                            e.tensor_scalar(out=o, in0=i, scalar1=s1, scalar2=-1.0,
                                            op0=OP.mult, op1=OP.mult),
                            waits=[dve_chain()], inc=("dve",))["dve"]
                ow = [("dve", nm)]
                if len(out_stores) >= 2:
                    ow.append(out_stores[-2])
                oa = act(af.ap("outb", pr * D, D), af.ap("r2", pr * D, D),
                         AF.Identity, af.ap("ln", 4, 1), af.ap("ln", 3, 1),
                         waits=ow)["act"]
                r2_guard[pr] = oa
                sv2 = d_sp("s_out%d" % pr, out_d[ts * 128:(ts + 1) * 128, :],
                           af.ap("outb", pr * D, D), waits=[("act", oa)])
                out_stores.append(sv2)
            th_barrier_dve = B.val("dve")
            th_barrier_act = B.val("act")

    finals = [(s, B.counts[s]) for s in list(B.counts)
              if s.startswith("s_") and B.counts[s] > 0]
    B.emit("sp", lambda e: e.nop(), waits=finals)
    B.emit("gp", lambda e: e.nop(), waits=finals)

    B.build_block()
    return nc, B



def _host_prep(inputs):
    x = np.ascontiguousarray(inputs["L0_attn_x"][0])
    Wq, bq = inputs["L0_attn_Wq"], inputs["L0_attn_bq"]
    Wk, bk = inputs["L0_attn_Wk"], inputs["L0_attn_bk"]
    Wv, bv = inputs["L0_attn_Wv"], inputs["L0_attn_bv"]
    Wo, bo = inputs["L0_attn_Wo"], inputs["L0_attn_bo"]
    lut = inputs["L0_ffn_lut"]
    g_idx = inputs["L0_ffn_gate_idx"]
    u_idx = inputs["L0_ffn_up_idx"]
    d_idx = inputs["L0_ffn_down_idx"]

    xT = np.ascontiguousarray(x.T)

    freqs = (10000.0 ** (-(np.arange(0, D, 2, dtype=np.float32) / D)))[:HD // 2]
    ang = np.outer(np.arange(T, dtype=np.float32), freqs).astype(np.float32)
    cosf = np.cos(ang).astype(np.float32).T
    sinf = np.sin(ang).astype(np.float32).T
    cosd = np.ascontiguousarray(np.concatenate([cosf, cosf], axis=0))
    sind = np.ascontiguousarray(np.concatenate([sinf, sinf], axis=0))

    kk = np.arange(128)[:, None]
    qq = np.arange(CH)[None, :]
    masks = np.concatenate(
        [((128 * j + kk) <= qq).astype(np.float32) for j in range(4)],
        axis=1).astype(bf16)

    perm = np.concatenate([np.arange(0, HD, 2), np.arange(1, HD, 2)])
    lutb = np.ascontiguousarray(np.tile(lut[None, :], (128, 1)).astype(np.float32))

    g_idxT = np.ascontiguousarray(g_idx.T)
    u_idxT = np.ascontiguousarray(u_idx.T)
    d_idxT = np.ascontiguousarray(d_idx.T)

    in_maps = []
    for c in range(NCORE):
        qh = [2 * c, 2 * c + 1]
        kvh = c // 2
        wq_c = np.empty((D, NH * HD), np.float32)
        bq_c = np.empty((NH * HD,), np.float32)
        for i, h in enumerate(qh):
            wq_c[:, i * HD:(i + 1) * HD] = Wq[:, h * HD + perm]
            bq_c[i * HD:(i + 1) * HD] = bq[h * HD + perm]
        bq_c = np.ascontiguousarray(bq_c.reshape(NH, HD).T)
        wk_c = np.ascontiguousarray(Wk[:, kvh * HD + perm])
        bk_c = np.ascontiguousarray(bk[kvh * HD + perm][:, None])
        wv_c = np.ascontiguousarray(Wv[:, kvh * HD:(kvh + 1) * HD])
        bv_c = np.ascontiguousarray(bv[kvh * HD:(kvh + 1) * HD][:, None])
        x_own_c = np.ascontiguousarray(
            np.concatenate([x[c * CH:(c + 1) * CH],
                            x[(8 + c) * CH:(9 + c) * CH]], axis=0))
        rg = slice(c * (D // NCORE), (c + 1) * (D // NCORE))
        rd = slice(c * (INTER // NCORE), (c + 1) * (INTER // NCORE))
        in_maps.append({
            "xT": xT, "x_own": x_own_c,
            "wq": np.ascontiguousarray(wq_c), "bq": bq_c,
            "wk": wk_c, "bk": bk_c, "wv": wv_c, "bv": bv_c,
            "wo": Wo, "bo": np.ascontiguousarray(bo[None, :]),
            "cosd": cosd, "sind": sind, "maskd": masks, "lutd": lutb,
            "gidx": np.ascontiguousarray(g_idxT[rg]).astype(bf16),
            "uidx": np.ascontiguousarray(u_idxT[rg]).astype(bf16),
            "didx": np.ascontiguousarray(d_idxT[rd]).astype(bf16),
        })
    return in_maps


def kernel(**inputs):
    in_maps = _host_prep(inputs)
    nc, _ = build_nc()
    res = run_bass_kernel_spmd(nc, in_maps, core_ids=list(range(NCORE)))
    out = np.empty((T, D), np.float32)
    for c in range(NCORE):
        oc = res.results[c]["out"]
        out[c * CH:(c + 1) * CH] = oc[:CH]
        out[(8 + c) * CH:(9 + c) * CH] = oc[CH:]
    return out[None, :, :]




# version 10
# speedup vs baseline: 1.4689x; 1.3212x over previous
import numpy as np
import ml_dtypes
import contextlib

import concourse.bass as bass
import concourse.mybir as mybir
from concourse.bass_utils import run_bass_kernel_spmd
from concourse.alu_op_type import AluOpType
from concourse.aot_env import allow_ici_all_to_all

F32 = mybir.dt.float32
BF16 = mybir.dt.bfloat16
AF = mybir.ActivationFunctionType
OP = AluOpType
bf16 = ml_dtypes.bfloat16

NCORE = 8
T = 8192
D = 2048
HD = 128
NH = 2
CH = 512
NCH = T // CH
INTER = 5504
NIT = INTER // 128
NDM = D // 128
TOK = 2 * CH
NTS = TOK // 128
EPS = 1e-5
OSCALE = float(D) ** -0.5
C_SHIFT = 0.0

ENGS = ("pe", "act", "dve", "gp", "sp")


class Builder:
    def __init__(self, nc):
        self.nc = nc
        self.ops = {e: [] for e in ENGS}
        self.sems = {}
        self.counts = {}
        self.es = contextlib.ExitStack()

    def sem(self, name):
        if name not in self.sems:
            self.sems[name] = self.es.enter_context(self.nc.semaphore(name))
            self.counts[name] = 0
        return self.sems[name]

    def val(self, name):
        return self.counts.get(name, 0)

    def emit(self, eng, fn, waits=(), inc=(), dma=False):
        step = 16 if dma else 1
        out_vals = {}
        incs = []
        for s in inc:
            self.sem(s)
            self.counts[s] += step
            out_vals[s] = self.counts[s]
            incs.append((self.sems[s], step))
        merged = {}
        for (s, v) in waits:
            if v > 0:
                merged[s] = max(merged.get(s, 0), v)
        wl = [(self.sem(s), v) for s, v in merged.items()]
        self.ops[eng].append((wl, fn, incs))
        return out_vals

    def build_block(self):
        ops = self.ops
        with self.nc.Block() as block:
            def runner(eng_name):
                def _run(eng):
                    for wl, fn, incs in ops[eng_name]:
                        for sem, v in wl:
                            eng.wait_ge(sem, v)
                        instr = fn(eng)
                        for sem, step in incs:
                            if instr is None:
                                instr = eng.nop()
                            instr = instr.then_inc(sem, step)
                return _run
            block.tensor(runner("pe"))
            block.scalar(runner("act"))
            block.vector(runner("dve"))
            block.gpsimd(runner("gp"))
            block.sync(runner("sp"))


class Arena:

    def __init__(self, tensor, ncols, align=16):
        self.t = tensor
        self.n = ncols
        self.align = align
        self.live = {}

    def alloc(self, name, cols):
        assert name not in self.live
        al = self.align
        taken = sorted(self.live.values())
        start = 0
        for s, ln in taken:
            if start + cols <= s:
                break
            start = -(-(s + ln) // al) * al
        assert start + cols <= self.n, (
            f"arena overflow: {name} needs {cols} @ {start}, cap {self.n}")
        self.live[name] = (start, cols)
        return start

    def free(self, *names):
        for nm in names:
            del self.live[nm]

    def ap(self, name, c0=0, cols=None, p0=0, pn=128):
        s, ln = self.live[name]
        if cols is None:
            cols = ln - c0
        assert c0 + cols <= ln, (name, c0, cols, ln)
        return self.t[p0:p0 + pn, s + c0:s + c0 + cols]


def build_nc():
    nc = bass.Bass("TRN2", target_bir_lowering=False)

    def din(name, shape, dt=F32):
        return nc.dram_tensor(name, shape, dt, kind="ExternalInput").ap()

    xT = din("xT", [D, T])
    x_own = din("x_own", [TOK, D])
    wq = din("wq", [D, NH * HD])
    wk = din("wk", [D, HD])
    wv = din("wv", [D, HD])
    wo = din("wo", [D, D])
    bq = din("bq", [128, NH])
    bk = din("bk", [HD, 1])
    bv = din("bv", [HD, 1])
    bo = din("bo", [1, D])
    cosd = din("cosd", [128, T])
    sind = din("sind", [128, T])
    maskd = din("maskd", [128, 4 * CH], BF16)
    lutd = din("lutd", [128, 16])
    gidx = din("gidx", [D // NCORE, INTER], BF16)
    uidx = din("uidx", [D // NCORE, INTER], BF16)
    didx = din("didx", [INTER // NCORE, D], BF16)
    out_d = nc.dram_tensor("out", [TOK, D], F32, kind="ExternalOutput").ap()

    ag_g_in = nc.dram_tensor("ag_g_in", [D // NCORE, INTER], BF16).ap()
    ag_u_in = nc.dram_tensor("ag_u_in", [D // NCORE, INTER], BF16).ap()
    ag_d_in = nc.dram_tensor("ag_d_in", [INTER // NCORE, D], BF16).ap()
    ag_g = nc.dram_tensor("ag_g", [D, INTER], BF16, addr_space="Shared").ap()
    ag_u = nc.dram_tensor("ag_u", [D, INTER], BF16, addr_space="Shared").ap()
    ag_d = nc.dram_tensor("ag_d", [INTER, D], BF16, addr_space="Shared").ap()
    a2a_in = [nc.dram_tensor(f"a2a_in{i}", [NCORE * NH * HD, CH], BF16).ap()
              for i in range(2)]
    a2a_out = [nc.dram_tensor(f"a2a_out{i}", [NCORE * NH * HD, CH], BF16).ap()
               for i in range(2)]

    B = Builder(nc)
    es = B.es

    BF_COLS = 81800
    F32_COLS = 12000
    abf = Arena(es.enter_context(nc.sbuf_tensor("abf", [128, BF_COLS], BF16)),
                BF_COLS, align=16)
    af = Arena(es.enter_context(nc.sbuf_tensor("af32", [128, F32_COLS], F32)),
               F32_COLS, align=8)
    pb = [es.enter_context(nc.psum_tensor(f"pb{i}", [128, 512], F32))
          for i in range(8)]

    af.alloc("lut", 16)
    af.alloc("bq", NH)
    af.alloc("bk", 1)
    af.alloc("bv", 1)
    af.alloc("bo_bc", D)
    af.alloc("sonesf", 128)
    af.alloc("ln", 16)
    af.alloc("epsv", 1)
    af.alloc("bn6", 24)
    abf.alloc("masks", 4 * CH)
    abf.alloc("sones", 1)

    def d_sp(sem, dst, src, waits=()):
        v = B.emit("sp", lambda e, dst=dst, src=src: e.dma_start(out=dst, in_=src),
                   waits=waits, inc=(sem,), dma=True)
        return (sem, v[sem])

    def d_gp(sem, dst, src, waits=()):
        v = B.emit("gp", lambda e, dst=dst, src=src: e.dma_start(out=dst, in_=src),
                   waits=waits, inc=(sem,), dma=True)
        return (sem, v[sem])

    def tsp(sem, dst, src, waits=()):
        v = B.emit("sp", lambda e, dst=dst, src=src: e.dma_start(
            out=dst, in_=src, transpose=True), waits=waits, inc=(sem,), dma=True)
        return (sem, v[sem])

    def mm(out, lhsT, rhs, start, stop, waits=(), inc=()):
        return B.emit("pe", lambda e, out=out, lhsT=lhsT, rhs=rhs, start=start,
                      stop=stop: e.matmul(out, lhsT, rhs, start=start, stop=stop),
                      waits=waits, inc=inc)

    def act(outp, inp, func, bias, scale, waits=()):
        return B.emit("act", lambda e, outp=outp, inp=inp, func=func, bias=bias,
                      scale=scale: e.activation(outp, inp, func, bias=bias,
                                                scale=scale),
                      waits=waits, inc=("act",))

    def tt(outp, in0, in1, op, waits=()):
        return B.emit("dve", lambda e, outp=outp, in0=in0, in1=in1, op=op:
                      e.tensor_tensor(out=outp, in0=in0, in1=in1, op=op),
                      waits=waits, inc=("dve",))

    def dve_chain():
        return ("dve", B.val("dve"))

    d_sp("s_ld", af.ap("lut"), lutd)
    d_sp("s_ld", af.ap("bq"), bq)
    d_sp("s_ld", af.ap("bk"), bk)
    d_sp("s_ld", af.ap("bv"), bv)
    d_sp("s_ld", abf.ap("masks"), maskd)
    ld0 = d_sp("s_ld", af.ap("bo_bc", 0, D, pn=1), bo)

    abf.alloc("xbuf", 2 * NDM * CH)
    abf.alloc("wq", NDM * NH * HD)
    abf.alloc("wk", NDM * HD)
    abf.alloc("wv", NDM * HD)
    af.alloc("cc", 2 * CH)
    af.alloc("ss", 2 * CH)
    af.alloc("ktmp", 2 * CH)
    af.alloc("qtmp", 4 * CH)
    af.alloc("ra", 2 * CH)
    af.alloc("rb", 2 * CH)
    abf.alloc("vbfT", 2 * CH)
    abf.alloc("kT", T)
    abf.alloc("qT", NH * T)
    abf.alloc("v", T)
    abf.alloc("p", 4 * CH)
    abf.alloc("o", 2 * CH)
    af.alloc("lrow", CH)
    af.alloc("lrec", CH)
    af.alloc("bc", 2 * CH)
    abf.alloc("idx", INTER)
    abf.alloc("wacc", INTER)
    abf.alloc("wcon", INTER)

    d_gp("s_ldw", abf.ap("wq"), wq.rearrange("(n p) m -> p n m", p=128))
    d_gp("s_ldw", abf.ap("wk"), wk.rearrange("(n p) m -> p n m", p=128))
    ldw = d_gp("s_ldw", abf.ap("wv"), wv.rearrange("(n p) m -> p n m", p=128))
    B.emit("dve", lambda e, a=abf.ap("sones"): e.memset(a, 1.0), inc=("dve",))
    B.emit("dve", lambda e, a=af.ap("epsv"): e.memset(a, EPS), inc=("dve",))
    ones_init = B.emit("dve", lambda e, a=af.ap("sonesf", pn=1): e.memset(a, OSCALE),
                       inc=("dve",))["dve"]

    def wtiles(nrows):
        tl, r = [], 0
        while r < nrows:
            tl.append((r, min(128, nrows - r)))
            r += 128
        return tl

    wb_done = {}
    gather_tiles = []
    for mat, idx_d, ag_in_d, fd in (("g", gidx, ag_g_in, INTER),
                                    ("u", uidx, ag_u_in, INTER),
                                    ("d", didx, ag_d_in, D)):
        tl = wtiles(idx_d.shape[0])
        for ti, (r0, nr) in enumerate(tl):
            gather_tiles.append((mat, idx_d, ag_in_d, fd, r0, nr,
                                 ti == len(tl) - 1))

    _gstate = {"prev_store": None, "emitted": 0}

    def emit_gather_tile(k):
        mat, idx_d, ag_in_d, fd, r0, nr, last = gather_tiles[k]
        idx_ap = abf.ap("idx", 0, fd, pn=nr)
        acc_ap = abf.ap("wacc", 0, fd, pn=nr)
        con_ap = abf.ap("wcon", 0, fd, pn=nr)
        ps = _gstate["prev_store"]
        lv = d_gp("s_idx", idx_ap, idx_d[r0:r0 + nr, :],
                  waits=[ps] if ps else [])
        B.emit("dve", lambda e, o=acc_ap, i=idx_ap, s2=af.ap("lut", 0, 1, pn=nr):
               e.tensor_scalar(out=o, in0=i, scalar1=0.0, scalar2=s2,
                               op0=OP.is_equal, op1=OP.mult),
               waits=[lv, ld0, dve_chain()], inc=("dve",))
        for j in range(1, 16):
            B.emit("dve", lambda e, o=con_ap, i=idx_ap, j=j,
                   s2=af.ap("lut", j, 1, pn=nr):
                   e.tensor_scalar(out=o, in0=i, scalar1=float(j), scalar2=s2,
                                   op0=OP.is_equal, op1=OP.mult),
                   waits=[dve_chain()], inc=("dve",))
            tt(acc_ap, acc_ap, con_ap, OP.add, waits=[dve_chain()])
        _gstate["prev_store"] = d_gp("s_wst_" + mat, ag_in_d[r0:r0 + nr, :],
                                     acc_ap, waits=[dve_chain()])
        if last:
            wb_done[mat] = _gstate["prev_store"]

    x_dma = {}
    pe_xfree = {}
    act_evac = {}
    kq_done = {}
    vT_done = {}

    for tc in range(NCH):
        par = tc % 2
        w = [("pe", pe_xfree[tc - 2])] if tc >= 2 else []
        x_dma[tc] = d_gp("s_x%d" % par, abf.ap("xbuf", par * NDM * CH, NDM * CH),
                         xT[:, tc * CH:(tc + 1) * CH].rearrange(
                             "(n p) m -> p n m", p=128), waits=w)
        csw = [("dve", kq_done[tc - 2])] if tc >= 2 else []
        d_sp("s_cs%d" % par, af.ap("cc", par * CH, CH),
             cosd[:, tc * CH:(tc + 1) * CH], waits=csw)
        cs_val = d_sp("s_cs%d" % par, af.ap("ss", par * CH, CH),
                      sind[:, tc * CH:(tc + 1) * CH])

        pew = [x_dma[tc], ldw]
        if tc >= 2:
            pew += [("act", act_evac[("k", tc - 2)]),
                    ("act", act_evac[("v", tc - 2)])]
        if tc >= 1:
            pew.append(("act", act_evac[("q1", tc - 1)]))
        for dm in range(NDM):
            iv = mm(pb[par][:, :], abf.ap("wk", dm * HD, HD),
                    abf.ap("xbuf", par * NDM * CH + dm * CH, CH),
                    start=(dm == 0), stop=(dm == NDM - 1),
                    waits=pew if dm == 0 else (),
                    inc=("pe",) if dm == NDM - 1 else ())
        pe_k = iv["pe"]
        for dm in range(NDM):
            iv = mm(pb[2 + par][:, :], abf.ap("wv", dm * HD, HD),
                    abf.ap("xbuf", par * NDM * CH + dm * CH, CH),
                    start=(dm == 0), stop=(dm == NDM - 1),
                    inc=("pe",) if dm == NDM - 1 else ())
        pe_v = iv["pe"]
        pe_q = {}
        for h in range(NH):
            for dm in range(NDM):
                iv = mm(pb[4 + h][:, :], abf.ap("wq", dm * NH * HD + h * HD, HD),
                        abf.ap("xbuf", par * NDM * CH + dm * CH, CH),
                        start=(dm == 0), stop=(dm == NDM - 1),
                        inc=("pe",) if dm == NDM - 1 else ())
            pe_q[h] = iv["pe"]
        pe_xfree[tc] = pe_q[NH - 1]

        roped_prev = kq_done.get(tc - 1, 0)
        act(af.ap("ktmp", 0, CH, p0=0, pn=64), pb[par][0:64, :], AF.Identity,
            af.ap("bk", p0=0, pn=64), 1.0,
            waits=[("pe", pe_k), ("dve", roped_prev), ld0])
        act_evac[("k", tc)] = act(
            af.ap("ktmp", CH, CH, p0=0, pn=64), pb[par][64:128, :], AF.Identity,
            af.ap("bk", p0=64, pn=64), 1.0)["act"]
        vw = [("pe", pe_v)]
        if tc >= 2:
            vw.append(vT_done[tc - 2])
        act_evac[("v", tc)] = act(abf.ap("vbfT", par * CH, CH), pb[2 + par][:, :],
                                  AF.Identity, af.ap("bv"), 1.0, waits=vw)["act"]
        for h in range(NH):
            act(af.ap("qtmp", h * 2 * CH, CH, p0=0, pn=64), pb[4 + h][0:64, :],
                AF.Identity, af.ap("bq", h, 1, p0=0, pn=64), 1.0,
                waits=[("pe", pe_q[h]), ("dve", roped_prev)])
            act_evac[(f"q{h}", tc)] = act(
                af.ap("qtmp", h * 2 * CH + CH, CH, p0=0, pn=64),
                pb[4 + h][64:128, :], AF.Identity,
                af.ap("bq", h, 1, p0=64, pn=64), 1.0)["act"]

        for j in range(CH // 128):
            vts = tsp("s_vt%d" % par, abf.ap("v", tc * CH + j * 128, 128),
                      abf.ap("vbfT", par * CH + j * 128, 128),
                      waits=[("act", act_evac[("v", tc)])] if j == 0 else ())
        vT_done[tc] = vts

        def rope(lo_ap, hi_ap, dst_name, dst_off, wait_act):
            t1 = af.ap("ra", par * CH, CH, pn=64)
            t2 = af.ap("rb", par * CH, CH, pn=64)
            cc64 = af.ap("cc", par * CH, CH, pn=64)
            ss64 = af.ap("ss", par * CH, CH, pn=64)
            tt(t1, lo_ap, cc64, OP.mult,
               waits=[("act", wait_act), cs_val, dve_chain()])
            tt(t2, hi_ap, ss64, OP.mult, waits=[dve_chain()])
            tt(abf.ap(dst_name, dst_off, CH, p0=0, pn=64), t1, t2, OP.subtract,
               waits=[dve_chain()])
            tt(t1, lo_ap, ss64, OP.mult, waits=[dve_chain()])
            tt(t2, hi_ap, cc64, OP.mult, waits=[dve_chain()])
            tt(abf.ap(dst_name, dst_off, CH, p0=64, pn=64), t1, t2, OP.add,
               waits=[dve_chain()])

        rope(af.ap("ktmp", 0, CH, pn=64), af.ap("ktmp", CH, CH, pn=64),
             "kT", tc * CH, act_evac[("k", tc)])
        for h in range(NH):
            rope(af.ap("qtmp", h * 2 * CH, CH, pn=64),
                 af.ap("qtmp", h * 2 * CH + CH, CH, pn=64),
                 "qT", h * T + tc * CH, act_evac[(f"q{h}", tc)])
        kq_done[tc] = B.val("dve")

    proj_done_dve = B.val("dve")
    vt_final = [("s_vt0", B.val("s_vt0")), ("s_vt1", B.val("s_vt1"))]

    bo_w = [("pe", pe_xfree[NCH - 1]), ("act", act_evac[("q1", NCH - 1)]),
            ("dve", ones_init), ld0]
    bo_bc_done = 0
    for c4 in range(4):
        pv = mm(pb[4][:, :], af.ap("sonesf", 0, 128, pn=1),
                af.ap("bo_bc", c4 * 512, 512, pn=1), True, True,
                waits=bo_w if c4 == 0 else [("dve", bo_bc_done)],
                inc=("pe",))["pe"]
        bo_bc_done = B.emit("dve", lambda e, o=af.ap("bo_bc", c4 * 512, 512),
                            i=pb[c4 * 0 + 4][:, :]:
                            e.tensor_scalar(out=o, in0=i, scalar1=float(D) ** 0.5,
                                            scalar2=None, op0=OP.mult,
                                            op1=OP.bypass),
                            waits=[("pe", pv), dve_chain()], inc=("dve",))["dve"]

    NKT = CH // 128
    o_done = {}
    pslot_free = {}
    bank_free_qk = {}
    pv_free = {}
    bc_free = {}
    l_free = 0
    bv_prev = 0
    bc_prev = 0
    o_store_seq = {}
    attn_barrier = [("dve", max(proj_done_dve, bo_bc_done)), ld0] + vt_final
    gg = 0
    seq = 0
    bank_free = {}
    slot_free = {}
    pending_norm = [None]

    def emit_norm(hp, seq_, qc_, h_, lv_):
        B.emit("dve", lambda e, o=af.ap("lrow", 0, CH, pn=1),
               i=pb[5][0:1, :]: e.tensor_copy(o, i),
               waits=[("pe", lv_), dve_chain()], inc=("dve",))
        lf = B.emit("dve", lambda e, o=af.ap("lrec", 0, CH, pn=1),
                    i=af.ap("lrow", 0, CH, pn=1): e.reciprocal(o, i),
                    waits=[dve_chain(), ("pe", _norm["bv_prev"])],
                    inc=("dve",))["dve"]
        _norm["l_free"] = lf
        bw = [("dve", lf), ("dve", _norm["bc_prev"])]
        bv_ = mm(pb[4][:, :], af.ap("sonesf", 0, 128, pn=1),
                 af.ap("lrec", 0, CH, pn=1), True, True, waits=bw,
                 inc=("pe",))["pe"]
        _norm["bv_prev"] = bv_
        cb = B.emit("dve", lambda e, o=af.ap("bc", hp * CH, CH),
                    i=pb[4][:, :]: e.tensor_copy(o, i),
                    waits=[("pe", bv_), dve_chain()], inc=("dve",))["dve"]
        _norm["bc_prev"] = cb
        ow = [("dve", cb)]
        if seq_ >= 2:
            ow.append(o_store_seq[seq_ - 2])
        om = tt(abf.ap("o", hp * CH, CH), pb[6 + hp][:, :],
                af.ap("bc", hp * CH, CH), OP.mult, waits=ow)["dve"]
        pv_free[hp] = om
        half = qc_ // (NCH // 2)
        dest = qc_ % (NCH // 2)
        ov = d_sp("s_o%d%d" % (half, hp),
                  a2a_in[half][dest * NH * HD + h_ * HD:
                               dest * NH * HD + (h_ + 1) * HD, :],
                  abf.ap("o", hp * CH, CH), waits=[("dve", om)])
        o_done[(h_, qc_)] = ov
        o_store_seq[seq_] = ov
        ngt = len(gather_tiles)
        want = min(ngt, (seq_ + 1) * ngt // 24)
        while _gstate["emitted"] < want:
            emit_gather_tile(_gstate["emitted"])
            _gstate["emitted"] += 1

    _norm = {"bv_prev": 0, "bc_prev": 0, "l_free": 0}
    for qc in range(NCH):
        for h in range(NH):
            hp = seq % 2
            ng2 = 2 * (qc + 1)
            nkt = 4 * (qc + 1)
            pend_avl = None
            lv = None
            for g in range(ng2 + 1):
                if g < ng2:
                    bp = gg % 2
                    w = list(attn_barrier) if gg == 0 and seq == 0 else []
                    if bp in bank_free:
                        w.append(("act", bank_free[bp]))
                    for j in range(2):
                        kt = 2 * g + j
                        qv = mm(pb[2 * bp + j][:, :],
                                abf.ap("kT", kt * 128, 128),
                                abf.ap("qT", h * T + qc * CH, CH), True, True,
                                waits=w if j == 0 else (),
                                inc=("pe",) if j == 1 else ())
                    qk_g = qv["pe"]
                    if g == 0 and pending_norm[0] is not None:
                        pending_norm[0]()
                        pending_norm[0] = None
                    ew = [("pe", qk_g)]
                    if bp in slot_free:
                        ew.append(("pe", slot_free[bp]))
                    for j in range(2):
                        avv = B.emit(
                            "act", lambda e, o=abf.ap("p", (2 * bp + j) * CH, CH),
                            i=pb[2 * bp + j][:, :]:
                            e.activation(o, i, AF.Exp, bias=-C_SHIFT, scale=1.0),
                            waits=ew if j == 0 else (), inc=("act",))["act"]
                    bank_free[bp] = avv
                    pdep = ("act", avv)
                    if 2 * g >= nkt - 4:
                        for j in range(2):
                            kt = 2 * g + j
                            mj = kt - (nkt - 4)
                            mv = tt(abf.ap("p", (2 * bp + j) * CH, CH),
                                    abf.ap("p", (2 * bp + j) * CH, CH),
                                    abf.ap("masks", mj * CH, CH), OP.mult,
                                    waits=[("act", avv), dve_chain()] if j == 0
                                    else [dve_chain()])["dve"]
                        pdep = ("dve", mv)
                if pend_avl is not None:
                    pg, ppdep = pend_avl
                    pbp = pg % 2
                    for j in range(2):
                        kt = 2 * (pg - gg0) + j
                        avw = []
                        if j == 0:
                            avw.append(ppdep)
                            if kt == 0 and hp in pv_free:
                                avw.append(("dve", pv_free[hp]))
                        mm(pb[6 + hp][:, :], abf.ap("v", kt * 128, 128),
                           abf.ap("p", (2 * pbp + j) * CH, CH),
                           start=(kt == 0), stop=(kt == nkt - 1), waits=avw)
                        lw = ([("dve", _norm["l_free"])]
                              if (kt == 0 and _norm["l_free"]) else [])
                        lvv = mm(pb[5][0:1, :], abf.ap("sones"),
                                 abf.ap("p", (2 * pbp + j) * CH, CH),
                                 start=(kt == 0), stop=(kt == nkt - 1),
                                 waits=lw, inc=("pe",) if j == 1 else ())
                        if j == 1:
                            lv = lvv["pe"]
                    slot_free[pbp] = lv
                if g < ng2:
                    if g == 0:
                        gg0 = gg
                    pend_avl = (gg, pdep)
                    gg += 1
            pending_norm[0] = (lambda hp=hp, s=seq, qc_=qc, h_=h, lv_=lv:
                               emit_norm(hp, s, qc_, h_, lv_))
            seq += 1
    pending_norm[0]()
    pending_norm[0] = None

    while _gstate["emitted"] < len(gather_tiles):
        emit_gather_tile(_gstate["emitted"])
        _gstate["emitted"] += 1

    cc_vals = {}

    def collective(kind, ins, outs, waits):
        def fn(e, kind=kind, ins=ins, outs=outs):
            if kind == "AllToAll":
                with allow_ici_all_to_all():
                    return e.collective_compute(
                        kind, OP.bypass, replica_groups=[list(range(NCORE))],
                        ins=ins, outs=outs)
            return e.collective_compute(
                kind, OP.bypass, replica_groups=[list(range(NCORE))],
                ins=ins, outs=outs)
        return B.emit("gp", fn, waits=waits, inc=("cc",))["cc"]

    cc_vals["g"] = collective("AllGather", [ag_g_in.opt()], [ag_g.opt()],
                              [wb_done["g"]])
    cc_vals["u"] = collective("AllGather", [ag_u_in.opt()], [ag_u.opt()],
                              [wb_done["u"]])
    cc_vals["a1"] = collective(
        "AllToAll", [a2a_in[0].opt()], [a2a_out[0].opt()],
        [("s_o00", B.val("s_o00")), ("s_o01", B.val("s_o01"))])
    cc_vals["a2"] = collective(
        "AllToAll", [a2a_in[1].opt()], [a2a_out[1].opt()],
        [("s_o10", B.val("s_o10")), ("s_o11", B.val("s_o11"))])
    cc_vals["d"] = collective("AllGather", [ag_d_in.opt()], [ag_d.opt()],
                              [wb_done["d"]])

    abf.free("xbuf", "wq", "wk", "wv", "vbfT", "kT", "qT", "v", "p", "o",
             "idx", "wacc", "wcon")
    af.free("cc", "ss", "ktmp", "qtmp", "ra", "rb", "lrow", "lrec", "bc")

    abf.alloc("oown", NDM * TOK)
    abf.alloc("woslab", 2 * D)
    abf.alloc("x1bf", NTS * D)
    abf.alloc("x1T", NDM * TOK)
    af.alloc("xbo", 2 * D)
    af.alloc("r1", 2 * D)

    for dt in range(NDM):
        for half in range(2):
            d_sp("s_ol", abf.ap("oown", dt * TOK + half * CH, CH),
                 a2a_out[half][dt * 128:(dt + 1) * 128, :],
                 waits=[("cc", cc_vals["a1" if half == 0 else "a2"])])
    oload_all = ("s_ol", B.val("s_ol"))

    NOROUND = NTS // 2
    wo_pe = {}
    evac_done = {}
    x1_done = {}
    xbo_add = {}

    for rnd in range(NOROUND):
        ts0, ts1 = 2 * rnd, 2 * rnd + 1
        for ts in (ts0, ts1):
            pr = ts % 2
            w = []
            if ts >= 2:
                w.append(("dve", evac_done[(rnd - 1, (ts % 2) * 4 + 3)]))
            xv = d_sp("s_xo%d" % pr, af.ap("xbo", pr * D, D),
                      x_own[ts * 128:(ts + 1) * 128, :], waits=w)
            xbo_add[ts] = tt(af.ap("xbo", pr * D, D), af.ap("xbo", pr * D, D),
                             af.ap("bo_bc"), OP.add,
                             waits=[xv, ("dve", max(B.val("dve"),
                                                    bo_bc_done))])["dve"]
        for dt in range(NDM):
            par = dt % 2
            w = []
            key_prev = (rnd, dt - 2) if dt >= 2 else (rnd - 1, dt + NDM - 2)
            if key_prev in wo_pe:
                w.append(("pe", wo_pe[key_prev]))
            gval = d_gp("s_wo%d" % par, abf.ap("woslab", par * D, D),
                        wo[dt * 128:(dt + 1) * 128, :], waits=w)
            for i, ts in enumerate((ts0, ts1)):
                mw = [gval]
                if dt == 0:
                    mw.append(oload_all)
                    for c4 in range(4):
                        if (rnd - 1, i * 4 + c4) in evac_done:
                            mw.append(("dve", evac_done[(rnd - 1, i * 4 + c4)]))
                for c4 in range(4):
                    iv = mm(pb[i * 4 + c4][:, :],
                            abf.ap("oown", dt * TOK + ts * 128, 128),
                            abf.ap("woslab", par * D + c4 * 512, 512),
                            start=(dt == 0), stop=(dt == NDM - 1),
                            waits=mw if c4 == 0 else (),
                            inc=("pe",) if (c4 == 3 and i == 1) else ())
            wo_pe[(rnd, dt)] = iv["pe"]
        mm_done = wo_pe[(rnd, NDM - 1)]
        for i, ts in enumerate((ts0, ts1)):
            pr = ts % 2
            for c4 in range(4):
                ev = tt(af.ap("r1", pr * D + c4 * 512, 512), pb[i * 4 + c4][:, :],
                        af.ap("xbo", pr * D + c4 * 512, 512), OP.add,
                        waits=[("pe", mm_done),
                               ("act", x1_done.get(ts - 2, 0)),
                               ("dve", max(B.val("dve"), xbo_add[ts]))])["dve"]
                evac_done[(rnd, i * 4 + c4)] = ev
            dvw = B.val("dve")
            for c4 in range(4):
                B.emit("dve", lambda e, o=af.ap("bn6", c4 * 6, 6),
                       i=af.ap("r1", pr * D + c4 * 512, 512): e.bn_stats(o, i),
                       waits=[("dve", dvw)] if c4 == 0 else (), inc=("dve",))
            B.emit("dve", lambda e, o=af.ap("ln", 0, 2), i=af.ap("bn6"):
                   e.bn_aggr(o, i), waits=[dve_chain()], inc=("dve",))
            sq = act(af.ap("ln", 2, 1), af.ap("ln", 1, 1), AF.Sqrt, af.ap("epsv"), 1.0,
                     waits=[dve_chain()])["act"]
            B.emit("dve", lambda e, o=af.ap("ln", 3, 1), i=af.ap("ln", 2, 1):
                   e.reciprocal(o, i), waits=[("act", sq), dve_chain()],
                   inc=("dve",))
            nm = B.emit("dve", lambda e, o=af.ap("ln", 4, 1), i=af.ap("ln", 0, 1),
                        s1=af.ap("ln", 3, 1):
                        e.tensor_scalar(out=o, in0=i, scalar1=s1, scalar2=-1.0,
                                        op0=OP.mult, op1=OP.mult),
                        waits=[dve_chain()], inc=("dve",))["dve"]
            x1_done[ts] = act(abf.ap("x1bf", ts * D, D), af.ap("r1", pr * D, D),
                              AF.Identity, af.ap("ln", 4, 1), af.ap("ln", 3, 1),
                              waits=[("dve", nm)])["act"]
            for dt in range(NDM):
                tv = tsp("s_x1t", abf.ap("x1T", dt * TOK + ts * 128, 128),
                         abf.ap("x1bf", ts * D + dt * 128, 128),
                         waits=[("act", x1_done[ts])] if dt == 0 else ())
    x1T_all = ("s_x1t", B.val("s_x1t"))

    abf.free("oown", "woslab")
    af.free("xbo", "r1")
    abf.alloc("hT", NIT * CH)
    abf.alloc("wgslab", 2 * D)
    abf.alloc("wuslab", 2 * D)
    abf.alloc("wdslab", 2 * D)
    af.alloc("sg", 2 * 512)
    af.alloc("r2", 2 * D)
    af.alloc("outb", 2 * D)

    out_stores = []
    r2_guard = {0: 0, 1: 0}
    gslab_pe = {}
    prev_down_pe = 0
    th_barrier_dve = 0
    th_barrier_act = 0
    for th in range(2):
        silu_done = {}
        hmul_done = {}
        for it in range(NIT):
            par = it % 2
            w = []
            if (th, it - 2) in gslab_pe:
                w.append(("pe", gslab_pe[(th, it - 2)]))
            elif (th - 1, NIT - 2 + it) in gslab_pe:
                w.append(("pe", gslab_pe[(th - 1, NIT - 2 + it)]))
            if th == 1 and it < 2:
                w.append(("pe", prev_down_pe))
            ccw = [("cc", cc_vals["g"])] if (th == 0 and it < 2) else []
            d_sp("s_wgu%d" % par, abf.ap("wgslab", par * D, D),
                 ag_g[:, it * 128:(it + 1) * 128].rearrange(
                     "(n p) m -> p n m", p=128), waits=w + ccw)
            ccw = [("cc", cc_vals["u"])] if (th == 0 and it < 2) else []
            slab_val = d_sp("s_wgu%d" % par, abf.ap("wuslab", par * D, D),
                            ag_u[:, it * 128:(it + 1) * 128].rearrange(
                                "(n p) m -> p n m", p=128), waits=ccw)
            mw = [slab_val, x1T_all]
            if th == 0 and it == 0:
                mw += [("dve", B.val("dve")), ("act", B.val("act"))]
            if th == 1 and it < 2:
                mw += [("dve", th_barrier_dve), ("act", th_barrier_act)]
            if (th, it - 2) in silu_done:
                mw += [("act", silu_done[(th, it - 2)]),
                       ("dve", hmul_done[(th, it - 2)])]
            for dm in range(NDM):
                mm(pb[par][:, :], abf.ap("wgslab", par * D + dm * 128, 128),
                   abf.ap("x1T", dm * TOK + th * CH, CH),
                   start=(dm == 0), stop=(dm == NDM - 1),
                   waits=mw if dm == 0 else ())
            for dm in range(NDM):
                iv = mm(pb[2 + par][:, :], abf.ap("wuslab", par * D + dm * 128, 128),
                        abf.ap("x1T", dm * TOK + th * CH, CH),
                        start=(dm == 0), stop=(dm == NDM - 1),
                        inc=("pe",) if dm == NDM - 1 else ())
            gmv = iv["pe"]
            gslab_pe[(th, it)] = gmv
            sw = [("pe", gmv)]
            if (th, it - 2) in hmul_done:
                sw.append(("dve", hmul_done[(th, it - 2)]))
            elif th == 1 and it < 2:
                sw.append(("dve", th_barrier_dve))
            silu_done[(th, it)] = act(af.ap("sg", par * 512, 512), pb[par][:, :],
                                      AF.Silu, 0.0, 1.0, waits=sw)["act"]
            hmul_done[(th, it)] = B.emit(
                "dve", lambda e, o=abf.ap("hT", it * CH, CH),
                i0=af.ap("sg", par * 512, 512), i1=pb[2 + par][:, :]:
                e.tensor_tensor(out=o, in0=i0, in1=i1, op=OP.mult),
                waits=[("act", silu_done[(th, it)]), dve_chain()],
                inc=("dve",))["dve"]
        hT_ready = max(hmul_done.values())
        last_silu = max(silu_done.values())

        dpair_pe = {}
        for pr2 in range(2):
            ts2a, ts2b = 2 * pr2, 2 * pr2 + 1
            for it in range(NIT):
                par = it % 2
                w = []
                if (pr2, it - 2) in dpair_pe:
                    w.append(("pe", dpair_pe[(pr2, it - 2)]))
                elif it < 2 and prev_down_pe:
                    w.append(("pe", prev_down_pe))
                ccw = ([("cc", cc_vals["d"])]
                       if (th == 0 and pr2 == 0 and it < 2) else [])
                sval = d_sp("s_wd%d" % par, abf.ap("wdslab", par * D, D),
                            ag_d[it * 128:(it + 1) * 128, :],
                            waits=w + ccw)
                mw = [sval, ("dve", hT_ready)]
                if it == 0:
                    mw += [("dve", B.val("dve")), ("act", B.val("act"))]
                for i2, ts2 in enumerate((ts2a, ts2b)):
                    for hc in range(4):
                        iv = mm(pb[i2 * 4 + hc][:, :],
                                abf.ap("hT", it * CH + ts2 * 128, 128),
                                abf.ap("wdslab", par * D + hc * 512, 512),
                                start=(it == 0), stop=(it == NIT - 1),
                                waits=mw if (i2 == 0 and hc == 0) else (),
                                inc=("pe",) if (i2 == 1 and hc == 3) else ())
                dpair_pe[(pr2, it)] = iv["pe"]
            down_done = dpair_pe[(pr2, NIT - 1)]
            prev_down_pe = down_done
            for i2, ts2 in enumerate((ts2a, ts2b)):
                ts = th * 4 + ts2
                pr = i2
                for hc in range(4):
                    tt(af.ap("r2", pr * D + hc * 512, 512), pb[i2 * 4 + hc][:, :],
                       abf.ap("x1bf", ts * D + hc * 512, 512), OP.add,
                       waits=[("pe", down_done), dve_chain(),
                              ("act", r2_guard[pr])])
                dvv = B.val("dve")
                for c4 in range(4):
                    B.emit("dve", lambda e, o=af.ap("bn6", c4 * 6, 6),
                           i=af.ap("r2", pr * D + c4 * 512, 512): e.bn_stats(o, i),
                           waits=[("dve", dvv)] if c4 == 0 else (), inc=("dve",))
                B.emit("dve", lambda e, o=af.ap("ln", 0, 2), i=af.ap("bn6"):
                       e.bn_aggr(o, i), waits=[dve_chain()], inc=("dve",))
                sq = act(af.ap("ln", 2, 1), af.ap("ln", 1, 1), AF.Sqrt, af.ap("epsv"), 1.0,
                         waits=[dve_chain()])["act"]
                B.emit("dve", lambda e, o=af.ap("ln", 3, 1), i=af.ap("ln", 2, 1):
                       e.reciprocal(o, i), waits=[("act", sq), dve_chain()],
                       inc=("dve",))
                nm = B.emit("dve", lambda e, o=af.ap("ln", 4, 1),
                            i=af.ap("ln", 0, 1), s1=af.ap("ln", 3, 1):
                            e.tensor_scalar(out=o, in0=i, scalar1=s1, scalar2=-1.0,
                                            op0=OP.mult, op1=OP.mult),
                            waits=[dve_chain()], inc=("dve",))["dve"]
                ow = [("dve", nm)]
                if len(out_stores) >= 2:
                    ow.append(out_stores[-2])
                oa = act(af.ap("outb", pr * D, D), af.ap("r2", pr * D, D),
                         AF.Identity, af.ap("ln", 4, 1), af.ap("ln", 3, 1),
                         waits=ow)["act"]
                r2_guard[pr] = oa
                sv2 = d_sp("s_out%d" % pr, out_d[ts * 128:(ts + 1) * 128, :],
                           af.ap("outb", pr * D, D), waits=[("act", oa)])
                out_stores.append(sv2)
            th_barrier_dve = B.val("dve")
            th_barrier_act = B.val("act")

    finals = [(s, B.counts[s]) for s in list(B.counts)
              if s.startswith("s_") and B.counts[s] > 0]
    B.emit("sp", lambda e: e.nop(), waits=finals)
    B.emit("gp", lambda e: e.nop(), waits=finals)

    B.build_block()
    return nc, B



def _host_prep(inputs):
    x = np.ascontiguousarray(inputs["L0_attn_x"][0])
    Wq, bq = inputs["L0_attn_Wq"], inputs["L0_attn_bq"]
    Wk, bk = inputs["L0_attn_Wk"], inputs["L0_attn_bk"]
    Wv, bv = inputs["L0_attn_Wv"], inputs["L0_attn_bv"]
    Wo, bo = inputs["L0_attn_Wo"], inputs["L0_attn_bo"]
    lut = inputs["L0_ffn_lut"]
    g_idx = inputs["L0_ffn_gate_idx"]
    u_idx = inputs["L0_ffn_up_idx"]
    d_idx = inputs["L0_ffn_down_idx"]

    xT = np.ascontiguousarray(x.T)

    freqs = (10000.0 ** (-(np.arange(0, D, 2, dtype=np.float32) / D)))[:HD // 2]
    ang = np.outer(np.arange(T, dtype=np.float32), freqs).astype(np.float32)
    cosf = np.cos(ang).astype(np.float32).T
    sinf = np.sin(ang).astype(np.float32).T
    cosd = np.ascontiguousarray(np.concatenate([cosf, cosf], axis=0))
    sind = np.ascontiguousarray(np.concatenate([sinf, sinf], axis=0))

    kk = np.arange(128)[:, None]
    qq = np.arange(CH)[None, :]
    masks = np.concatenate(
        [((128 * j + kk) <= qq).astype(np.float32) for j in range(4)],
        axis=1).astype(bf16)

    perm = np.concatenate([np.arange(0, HD, 2), np.arange(1, HD, 2)])
    lutb = np.ascontiguousarray(np.tile(lut[None, :], (128, 1)).astype(np.float32))

    g_idxT = np.ascontiguousarray(g_idx.T)
    u_idxT = np.ascontiguousarray(u_idx.T)
    d_idxT = np.ascontiguousarray(d_idx.T)

    in_maps = []
    for c in range(NCORE):
        qh = [2 * c, 2 * c + 1]
        kvh = c // 2
        wq_c = np.empty((D, NH * HD), np.float32)
        bq_c = np.empty((NH * HD,), np.float32)
        for i, h in enumerate(qh):
            wq_c[:, i * HD:(i + 1) * HD] = Wq[:, h * HD + perm]
            bq_c[i * HD:(i + 1) * HD] = bq[h * HD + perm]
        bq_c = np.ascontiguousarray(bq_c.reshape(NH, HD).T)
        wk_c = np.ascontiguousarray(Wk[:, kvh * HD + perm])
        bk_c = np.ascontiguousarray(bk[kvh * HD + perm][:, None])
        wv_c = np.ascontiguousarray(Wv[:, kvh * HD:(kvh + 1) * HD])
        bv_c = np.ascontiguousarray(bv[kvh * HD:(kvh + 1) * HD][:, None])
        x_own_c = np.ascontiguousarray(
            np.concatenate([x[c * CH:(c + 1) * CH],
                            x[(8 + c) * CH:(9 + c) * CH]], axis=0))
        rg = slice(c * (D // NCORE), (c + 1) * (D // NCORE))
        rd = slice(c * (INTER // NCORE), (c + 1) * (INTER // NCORE))
        in_maps.append({
            "xT": xT, "x_own": x_own_c,
            "wq": np.ascontiguousarray(wq_c), "bq": bq_c,
            "wk": wk_c, "bk": bk_c, "wv": wv_c, "bv": bv_c,
            "wo": Wo, "bo": np.ascontiguousarray(bo[None, :]),
            "cosd": cosd, "sind": sind, "maskd": masks, "lutd": lutb,
            "gidx": np.ascontiguousarray(g_idxT[rg]).astype(bf16),
            "uidx": np.ascontiguousarray(u_idxT[rg]).astype(bf16),
            "didx": np.ascontiguousarray(d_idxT[rd]).astype(bf16),
        })
    return in_maps


def kernel(**inputs):
    in_maps = _host_prep(inputs)
    nc, _ = build_nc()
    res = run_bass_kernel_spmd(nc, in_maps, core_ids=list(range(NCORE)))
    out = np.empty((T, D), np.float32)
    for c in range(NCORE):
        oc = res.results[c]["out"]
        out[c * CH:(c + 1) * CH] = oc[:CH]
        out[(8 + c) * CH:(9 + c) * CH] = oc[CH:]
    return out[None, :, :]


